# version 14
# baseline (speedup 1.0000x reference)
import sys
if "/opt/trn_rl_repo" not in sys.path:
    sys.path.insert(0, "/opt/trn_rl_repo")

from contextlib import ExitStack

import numpy as np

import concourse.bass as bass
import concourse.mybir as mybir
from concourse.bass_utils import run_bass_kernel_spmd

F32 = mybir.dt.float32
BF16 = mybir.dt.bfloat16
AF = mybir.ActivationFunctionType
OP = mybir.AluOpType
AX = mybir.AxisListType.X

P = 128
CG = 8
TT = 256
NSEQ = 2
TSEQ = 4096
NTPS = TSEQ // TT
NT = NSEQ * NTPS
C = 1024
NCORES = 8

INV_C = 1.0 / 1024.0
CTAU = (128.0 - 1e-5) / 128.0
I_TMK, I_TMV, I_TMR, I_TD, I_TF, I_LNG, I_LNB, I_FG, I_FB = range(9)


def _ap(handle, off, dims):
    a = handle.ap()
    return bass.AP(a.tensor, off, [list(d) for d in dims])


def _fl(ap3):
    return ap3.rearrange("p a b -> p (a b)")


def _bcast_t(col_ap, n):
    return bass.AP(col_ap.tensor, col_ap.offset, [list(col_ap.ap[0]), [0, n]])


def build_nc():
    nc = bass.Bass("TRN2", target_bir_lowering=False, detect_race_conditions=False)

    x_d = nc.dram_tensor("x", [NSEQ, C, TSEQ], F32, kind="ExternalInput")
    wkt_d = nc.dram_tensor("wkt", [C, C], F32, kind="ExternalInput")
    wvt_d = nc.dram_tensor("wvt", [C, C], F32, kind="ExternalInput")
    wrt_d = nc.dram_tensor("wrt", [C, C], F32, kind="ExternalInput")
    wot_d = nc.dram_tensor("wot", [C, C], F32, kind="ExternalInput")
    wffn_d = nc.dram_tensor("wffn", [C, C], F32, kind="ExternalInput")
    co_d = nc.dram_tensor("consts", [9, P, CG], F32, kind="ExternalInput")
    out_d = nc.dram_tensor("out", [NSEQ, C, TSEQ], F32, kind="ExternalOutput")

    sc_part = nc.dram_tensor("sc_part", [NT, P, CG * TT], F32)
    sc_xn2 = nc.dram_tensor("sc_xn2", [NT, P, CG * TT], BF16)
    colbuf = nc.dram_tensor("colbuf", [P], F32)
    cc_in = nc.dram_tensor("cc_in", [1], F32)
    cc_out = nc.dram_tensor("cc_out", [1], F32)

    with ExitStack() as es:
        e = es.enter_context
        wk_sb = e(nc.sbuf_tensor("wk_sb", [P, CG, C], BF16))
        wv_sb = e(nc.sbuf_tensor("wv_sb", [P, CG, C], BF16))
        wr_sb = e(nc.sbuf_tensor("wr_sb", [P, CG, C], BF16))
        wo_sb = e(nc.sbuf_tensor("wo_sb", [P, CG, C], BF16))
        wq_sb = e(nc.sbuf_tensor("wq_sb", [P, CG, C], BF16))
        stage = e(nc.sbuf_tensor("stage", [P, C], F32))
        xT = e(nc.sbuf_tensor("xT", [P, 2, CG, TT], F32))
        xbf = e(nc.sbuf_tensor("xbf", [P, CG, TT], BF16))
        xn2b = e(nc.sbuf_tensor("xn2b", [P, 2, CG, TT], BF16))
        partial = e(nc.sbuf_tensor("partial", [P, 2, CG, TT], F32))
        big = e(nc.sbuf_tensor("big", [P, 2, CG, TT], F32))
        xn = e(nc.sbuf_tensor("xn", [P, CG, TT], BF16))
        tmpb = e(nc.sbuf_tensor("tmpb", [P, CG, TT], BF16))
        xxs = e(nc.sbuf_tensor("xxs", [P, CG, TT + 2], BF16))
        xk = e(nc.sbuf_tensor("xk", [P, CG, TT], BF16))
        xv = e(nc.sbuf_tensor("xv", [P, CG, TT], BF16))
        xr = e(nc.sbuf_tensor("xr", [P, CG, TT], BF16))
        obuf = e(nc.sbuf_tensor("obuf", [P, CG, TT], BF16))
        vsb = e(nc.sbuf_tensor("vsb", [P, CG, TT], BF16))
        sqA = e(nc.sbuf_tensor("sqA", [P, CG, TT], BF16))
        sqO = e(nc.sbuf_tensor("sqO", [P, CG, TT], BF16))
        xln2 = e(nc.sbuf_tensor("xln2", [P, CG, TT], BF16))
        sums = e(nc.sbuf_tensor("sums", [P, 2, TT], F32))
        wsp = e(nc.sbuf_tensor("wsp", [P, 4, TT], F32))
        r13 = e(nc.sbuf_tensor("r13", [P, 2, TT], F32))
        inv13 = e(nc.sbuf_tensor("inv13", [P, 2, TT], F32))
        a1b = e(nc.sbuf_tensor("a1b", [P, TT], BF16))
        b1mb = e(nc.sbuf_tensor("b1mb", [P, TT], BF16))
        A3b = e(nc.sbuf_tensor("A3b", [P, TT], BF16))
        B3mb = e(nc.sbuf_tensor("B3mb", [P, TT], BF16))
        a2b = e(nc.sbuf_tensor("a2b", [P, TT], BF16))
        b2mb = e(nc.sbuf_tensor("b2mb", [P, TT], BF16))
        co = e(nc.sbuf_tensor("co", [P, 9, CG], F32))
        dec_sb = e(nc.sbuf_tensor("dec_sb", [P, CG], F32))
        ones128 = e(nc.sbuf_tensor("ones128", [P, P], BF16))
        onesrow = e(nc.sbuf_tensor("onesrow", [1, P], F32))
        carry = e(nc.sbuf_tensor("carry", [P, CG], F32))
        xnl2 = e(nc.sbuf_tensor("xnl2", [P, CG, 2], BF16))
        zb2 = e(nc.sbuf_tensor("zb2", [P, CG, 2], BF16))
        mxacc = e(nc.sbuf_tensor("mxacc", [P, 1], F32))
        mxall = e(nc.sbuf_tensor("mxall", [P, NT], F32))
        absrow = e(nc.sbuf_tensor("absrow", [P, 1], F32))
        absall = e(nc.sbuf_tensor("absall", [P, CG], F32))
        thrb = e(nc.sbuf_tensor("thrb", [P, 2], F32))
        thrneg = e(nc.sbuf_tensor("thrneg", [P, 1], F32))
        tauv = e(nc.sbuf_tensor("tauv", [P, 2], F32))
        mrow = e(nc.sbuf_tensor("mrow", [1, P], F32))
        tiny = e(nc.sbuf_tensor("tiny", [1, 8], F32))

        ps_stat = e(nc.psum_tensor("ps_stat", [P, 512], F32))
        ps_k0 = e(nc.psum_tensor("ps_k0", [P, 512], F32))
        ps_k1 = e(nc.psum_tensor("ps_k1", [P, 512], F32))
        ps_v0 = e(nc.psum_tensor("ps_v0", [P, 512], F32))
        ps_v1 = e(nc.psum_tensor("ps_v1", [P, 512], F32))
        ps_ra0 = e(nc.psum_tensor("ps_ra0", [P, 512], F32))
        ps_ra1 = e(nc.psum_tensor("ps_ra1", [P, 512], F32))
        ps_small = e(nc.psum_tensor("ps_small", [P, 512], F32))
        ps_k = [ps_k0, ps_k1]
        ps_v = [ps_v0, ps_v1]
        ps_ra = [ps_ra0, ps_ra1]

        names = ("X W WG B1 L2 SX SP SO G1 G2 CC BG "
                 "S1 S2 S3 S5 S6 S7 S8 SB SD T1 T1A T2 T3 T3A T4 T5 T6 TB VE1 VE2 "
                 "V0 V1 V2 V3 V4 V5 V6 V7 V8 V9 VO VBX VG1 VG2 VT B2 B4 B5 B6").split()
        sem = {n: e(nc.semaphore(n)) for n in names}
        block = e(nc.Block())

        def w(eng, name, val):
            if val > 0:
                eng.wait_ge(sem[name], val)

        def col(k, cg):
            return co[:, k, cg:cg + 1]

        def bc(tile):
            a = tile[:, :]
            return bass.AP(a.tensor, a.offset, [[a.ap[0][0], P], [0, CG], [1, TT]])

        def x_tile_ap(i):
            seq, ti = divmod(i, NTPS)
            off = seq * C * TSEQ + ti * TT
            return _ap(x_d, off, [[TSEQ, P], [P * TSEQ, CG], [1, TT]])

        def out_tile_ap(i):
            seq, ti = divmod(i, NTPS)
            off = seq * C * TSEQ + ti * TT
            return _ap(out_d, off, [[TSEQ, P], [P * TSEQ, CG], [1, TT]])

        def scr_ap(handle, i):
            return _ap(handle, i * P * CG * TT, [[CG * TT, P], [1, CG * TT]])

        def wchunk_ap(handle, c_):
            return _ap(handle, c_ * P * C, [[C, P], [1, C]])

        @block.sync
        def _(sync):
            sync.dma_start(out=co[:, :, :],
                           in_=_ap(co_d, 0, [[CG, P], [P * CG, 9], [1, CG]])
                           ).then_inc(sem["W"], 16)
            for i in range(NT):
                w(sync, "V6", 8 * (i - 1))
                sync.dma_start(out=xT[:, i % 2, :, :], in_=x_tile_ap(i)
                               ).then_inc(sem["X"], 16)
            sync.wait_ge(sem["SX"], 16 * NT)
            sync.wait_ge(sem["SP"], 16 * NT)
            for i in range(NT):
                w(sync, "V8", i - 1)
                sync.dma_start(out=_fl(xn2b[:, i % 2, :, :]), in_=scr_ap(sc_xn2, i)
                               ).then_inc(sem["L2"], 16)
                w(sync, "V9", 8 * (i - 1))
                sync.dma_start(out=_fl(partial[:, i % 2, :, :]), in_=scr_ap(sc_part, i)
                               ).then_inc(sem["L2"], 16)

        @block.gpsimd
        def _(g):
            for wsb, wd in ((wk_sb, wkt_d), (wv_sb, wvt_d), (wr_sb, wrt_d),
                            (wo_sb, wot_d)):
                g.dma_start(out=wsb[:, :, :],
                            in_=_ap(wd, 0, [[C, P], [P * C, CG], [1, C]])
                            ).then_inc(sem["WG"], 16)
            for c_ in range(CG):
                w(g, "B2", c_)
                g.dma_start(out=stage[:, :], in_=wchunk_ap(wffn_d, c_)
                            ).then_inc(sem["B1"], 16)
            g.wait_ge(sem["B2"], CG + 1)
            g.dma_start(out=colbuf.ap(), in_=absrow[:, 0:1]).then_inc(sem["BG"], 16)
            g.wait_ge(sem["BG"], 16)
            g.dma_start(out=mrow[0:1, :],
                        in_=_ap(colbuf, 0, [[P, 1], [1, P]])).then_inc(sem["BG"], 16)
            for c_ in range(CG):
                w(g, "B6", c_)
                g.dma_start(out=stage[:, :], in_=wchunk_ap(wffn_d, c_)
                            ).then_inc(sem["B1"], 16)
            def win3():
                a = xxs[:, 0, 1:TT + 1]
                return bass.AP(a.tensor, a.offset,
                               [list(a.ap[0]), [TT + 2, CG], [1, TT]])

            def tmix3(kcol):
                a = co[:, kcol, 0:1]
                return bass.AP(a.tensor, a.offset,
                               [list(a.ap[0]), [1, CG], [0, TT]])

            for i in range(NT):
                w(g, "VBX", i + 1)
                w(g, "T1", i + 1)
                g.tensor_sub(sqA[:, :, :], xn[:, :, :], win3())
                for kcol, buf in ((I_TMK, xk), (I_TMV, xv), (I_TMR, xr)):
                    g.tensor_mul(buf[:, :, :], sqA[:, :, :], tmix3(kcol))
                    mi = g.tensor_add(buf[:, :, :], buf[:, :, :], win3())
                mi.then_inc(sem["V2"], 1)
                w(g, "V7", i + 1)
                g.dma_start(out=scr_ap(sc_xn2, i), in_=_fl(xn2b[:, i % 2, :, :])
                            ).then_inc(sem["SX"], 16)
                w(g, "V6", 8 * (i + 1))
                g.dma_start(out=scr_ap(sc_part, i), in_=_fl(partial[:, i % 2, :, :])
                            ).then_inc(sem["SP"], 16)
            g.wait_ge(sem["VG1"], 1)
            g.dma_start(out=colbuf.ap(), in_=mxacc[:, 0:1]).then_inc(sem["G1"], 16)
            g.wait_ge(sem["G1"], 16)
            g.dma_start(out=mrow[0:1, :],
                        in_=_ap(colbuf, 0, [[P, 1], [1, P]])).then_inc(sem["G1"], 16)
            g.wait_ge(sem["VG2"], 1)
            g.dma_start(out=cc_in.ap(), in_=tiny[0:1, 3:4]).then_inc(sem["G1"], 16)
            g.wait_ge(sem["G1"], 48)
            g.collective_compute(
                "AllReduce", OP.max,
                replica_groups=[list(range(NCORES))],
                ins=[cc_in.ap()],
                outs=[cc_out.ap()],
            ).then_inc(sem["CC"], 1)
            g.wait_ge(sem["CC"], 1)
            g.dma_start(out=tiny[0:1, 4:5], in_=cc_out.ap()).then_inc(sem["G2"], 16)
            for i in range(NT):
                w(g, "V9", 8 * (i + 1))
                g.dma_start(out=out_tile_ap(i), in_=big[:, i % 2, :, :]
                            ).then_inc(sem["SO"], 16)
            g.wait_ge(sem["SO"], 16 * NT)

        @block.scalar
        def _(s):
            s.preload_act_table(AF.Exp)
            s.wait_ge(sem["W"], 16)
            s.activation(dec_sb[:, :], co[:, I_TD, :], AF.Exp
                         ).then_inc(sem["SD"], 1)
            s.preload_act_table(AF.Sqrt)
            s.wait_ge(sem["TB"], 1)
            s.copy(thrb[:, :], ps_small[:, 2:4]).then_inc(sem["SB"], 1)

            def s_cast(i):
                w(s, "X", 16 * (i + 1))
                w(s, "T1", i)
                w(s, "V7", i)
                s.copy(_fl(xbf[:, :, :]), _fl(xT[:, i % 2, :, :])
                       ).then_inc(sem["S1"], 1)

            def s_sqrt13(i):
                w(s, "V1", i + 1)
                s.activation(_fl(inv13[:, :, :]), _fl(r13[:, :, :]), AF.Sqrt
                             ).then_inc(sem["S3"], 1)

            def s_sqrt2(i):
                w(s, "V4", i + 1)
                s.activation(inv13[:, 0, :], r13[:, 0, :], AF.Sqrt
                             ).then_inc(sem["S6"], 1)

            s_cast(0)
            s_sqrt13(0)
            for i in range(NT - 1):
                s_cast(i + 1)
                s_sqrt13(i + 1)
                s_sqrt2(i)
            s_sqrt2(NT - 1)
            s.wait_ge(sem["T6"], 1)
            s.copy(tauv[:, :], ps_small[:, 0:2]).then_inc(sem["S7"], 1)

        @block.tensor
        def _(t):
            t.wait_ge(sem["B4"], 1)
            t.matmul(ps_small[:, 2:4], onesrow[0:1, :], tiny[0:1, 1:3],
                     start=True, stop=True).then_inc(sem["TB"], 1)
            t.wait_ge(sem["WG"], 64)

            def t_stats1(j):
                w(t, "S1", j + 1)
                w(t, "V0", j + 1)
                w(t, "S5", j - 1)
                for cg in range(CG):
                    mm = t.matmul(ps_stat[:, 0:TT], ones128[:, :], xbf[:, cg, :],
                                  start=(cg == 0), stop=(cg == CG - 1))
                mm.then_inc(sem["T1A"], 1)
                w(t, "VE1", j + 1)
                for cg in range(CG):
                    mm = t.matmul(ps_stat[:, 0:TT], ones128[:, :], sqA[:, cg, :],
                                  start=(cg == 0), stop=(cg == CG - 1))
                mm.then_inc(sem["T1"], 1)

            def t_proj(j):
                w(t, "V2", j + 1)
                for d in range(CG):
                    gidx = CG * j + d
                    sl = gidx % 2
                    w(t, "V3", gidx - 1)
                    if j >= 1:
                        w(t, "V6", 8 * j - 1 + sl)
                    for cg in range(CG):
                        t.matmul(ps_k[sl][:, 0:TT], wk_sb[:, cg, d * P:(d + 1) * P],
                                 xk[:, cg, :], start=(cg == 0), stop=(cg == CG - 1))
                    for cg in range(CG):
                        t.matmul(ps_v[sl][:, 0:TT], wv_sb[:, cg, d * P:(d + 1) * P],
                                 xv[:, cg, :], start=(cg == 0), stop=(cg == CG - 1))
                    for cg in range(CG):
                        mm = t.matmul(ps_ra[sl][:, 0:TT], wr_sb[:, cg, d * P:(d + 1) * P],
                                      xr[:, cg, :], start=(cg == 0), stop=(cg == CG - 1))
                    mm.then_inc(sem["T2"], 1)

            def t_stats2(i):
                w(t, "VO", i + 1)
                for cg in range(CG):
                    mm = t.matmul(ps_stat[:, 0:TT], ones128[:, :], obuf[:, cg, :],
                                  start=(cg == 0), stop=(cg == CG - 1))
                mm.then_inc(sem["T3A"], 1)
                w(t, "VE2", i + 1)
                for cg in range(CG):
                    mm = t.matmul(ps_stat[:, 0:TT], ones128[:, :], sqO[:, cg, :],
                                  start=(cg == 0), stop=(cg == CG - 1))
                mm.then_inc(sem["T3"], 1)

            def t_wo(i):
                w(t, "V5", i + 1)
                for d in range(CG):
                    gidx = CG * i + d
                    sl = gidx % 2
                    w(t, "V6", gidx - 1)
                    for cg in range(CG):
                        mm = t.matmul(ps_ra[sl][:, 0:TT], wo_sb[:, cg, d * P:(d + 1) * P],
                                      xln2[:, cg, :], start=(cg == 0), stop=(cg == CG - 1))
                    mm.then_inc(sem["T4"], 1)

            t_stats1(0)
            t_proj(0)
            for i in range(NT - 1):
                t_stats1(i + 1)
                t_stats2(i)
                t_wo(i)
                t_proj(i + 1)
            t_stats2(NT - 1)
            t_wo(NT - 1)
            t.wait_ge(sem["VT"], 1)
            t.matmul(ps_small[:, 0:2], onesrow[0:1, :], tiny[0:1, 5:7],
                     start=True, stop=True).then_inc(sem["T6"], 1)
            t.wait_ge(sem["V6"], 8 * NT)
            for i in range(NT):
                w(t, "V8", i + 1)
                for u in range(CG):
                    gidx = CG * i + u
                    sl = gidx % 2
                    w(t, "V9", gidx - 1)
                    for dcg in range(CG):
                        mm = t.matmul(ps_ra[sl][:, 0:TT], wq_sb[:, dcg, u * P:(u + 1) * P],
                                      xbf[:, dcg, :], start=(dcg == 0), stop=(dcg == CG - 1))
                    mm.then_inc(sem["T5"], 1)

        @block.vector
        def _(v):
            v.memset(ones128[:, :], 1.0)
            v.memset(onesrow[0:1, :], 1.0)
            v.memset(zb2[:, :, :], 0.0)
            v.memset(mxall[:, :], 0.0)
            for c_ in range(CG):
                w(v, "B1", 16 * (c_ + 1))
                v.tensor_reduce(absall[:, c_:c_ + 1], stage[:, :], AX, OP.add,
                                apply_absolute_value=True).then_inc(sem["B2"], 1)
            v.drain()
            v.tensor_reduce(absrow[:, :], absall[:, :], AX, OP.add)
            v.drain()
            v.sem_inc(sem["B2"], 1)
            v.wait_ge(sem["BG"], 32)
            v.tensor_reduce(tiny[0:1, 0:1], mrow[0:1, :], AX, OP.add)
            v.drain()
            v.tensor_scalar_mul(tiny[0:1, 7:8], tiny[0:1, 0:1], 0.5 / 1048576.0)
            v.tensor_scalar_mul(tiny[0:1, 2:3], tiny[0:1, 0:1], 1.0 / 1048576.0)
            v.drain()
            v.tensor_scalar_add(tiny[0:1, 1:2], tiny[0:1, 7:8], 5e-6)
            v.drain()
            v.sem_inc(sem["B4"], 1)
            v.wait_ge(sem["SB"], 1)
            v.tensor_scalar_mul(thrneg[:, :], thrb[:, 0:1], -1.0)
            v.drain()
            for c_ in range(CG):
                w(v, "B1", 16 * (CG + c_ + 1))
                gv = xk[:, 0:4, :].rearrange("p a b -> p (a b)")
                lv = xk[:, 4:8, :].rearrange("p a b -> p (a b)")
                v.tensor_single_scalar(gv, stage[:, :], thrb[:, 0:1], OP.is_gt)
                v.tensor_single_scalar(lv, stage[:, :], thrneg[:, 0:1], OP.is_lt)
                v.tensor_sub(wq_sb[:, c_, :], gv, lv).then_inc(sem["B6"], 1)
            v.wait_ge(sem["W"], 16)
            v.wait_ge(sem["SD"], 1)

            def v_A(i):
                w(v, "S1", i + 1)
                w(v, "T1", i)
                v.tensor_mul(_fl(sqA[:, :, :]), _fl(xbf[:, :, :]), _fl(xbf[:, :, :])
                             ).then_inc(sem["V0"], 1)
                w(v, "T1A", i + 1)
                v.tensor_copy(sums[:, 0, :], ps_stat[:, 0:TT]).then_inc(sem["VE1"], 1)
                w(v, "T1", i + 1)
                v.tensor_copy(sums[:, 1, :], ps_stat[:, 0:TT]).then_inc(sem["S2"], 1)
                v.tensor_scalar_mul(wsp[:, 0, :], sums[:, 0, :], INV_C)
                v.tensor_scalar_mul(wsp[:, 1, :], sums[:, 1, :], INV_C)
                v.tensor_mul(wsp[:, 2, :], wsp[:, 0, :], wsp[:, 0, :])
                v.tensor_sub(wsp[:, 1, :], wsp[:, 1, :], wsp[:, 2, :])
                v.tensor_scalar_add(wsp[:, 2, :], wsp[:, 1, :], 1e-5)
                v.reciprocal(r13[:, 0, :], wsp[:, 2, :])
                v.tensor_mul(wsp[:, 3, :], wsp[:, 1, :], r13[:, 0, :])
                v.tensor_scalar_add(wsp[:, 3, :], wsp[:, 3, :], 1e-3)
                v.reciprocal(r13[:, 1, :], wsp[:, 3, :]).then_inc(sem["V1"], 1)
                w(v, "S3", i + 1)
                v.tensor_copy(a1b[:, :], inv13[:, 0, :])
                v.tensor_mul(wsp[:, 2, :], wsp[:, 0, :], inv13[:, 0, :])
                v.tensor_copy(b1mb[:, :], wsp[:, 2, :])
                v.tensor_mul(A3b[:, :], inv13[:, 0, :], inv13[:, 1, :])
                v.tensor_mul(B3mb[:, :], wsp[:, 2, :], inv13[:, 1, :])

            def v_B(i):
                ti = i % NTPS
                v.tensor_mul(_fl(tmpb[:, :, :]), _fl(xbf[:, :, :]), bc(a1b))
                v.tensor_sub(_fl(xn[:, :, :]), _fl(tmpb[:, :, :]), bc(b1mb))
                w(v, "SX", 16 * (i - 1))
                v.tensor_mul(_fl(tmpb[:, :, :]), _fl(xbf[:, :, :]), bc(A3b))
                v.tensor_sub(_fl(tmpb[:, :, :]), _fl(tmpb[:, :, :]), bc(B3mb))
                for cg in range(CG):
                    mi = v.tensor_scalar(xn2b[:, i % 2, cg, :], tmpb[:, cg, :],
                                         col(I_FG, cg), col(I_FB, cg), OP.mult, OP.add)
                mi.then_inc(sem["V7"], 1)
                v.tensor_reduce(mxall[:, i:i + 1], _fl(xn2b[:, i % 2, :, :]), AX,
                                OP.max, apply_absolute_value=True)
                bsrc = zb2 if ti == 0 else xnl2
                for cg in range(CG):
                    v.tensor_copy(xxs[:, cg, 0:2], bsrc[:, cg, :])
                v.drain()
                for cg in range(CG):
                    v.tensor_copy(xxs[:, cg, 2:TT + 2], xn[:, cg, 0:TT])
                ln = xn[:, :, TT - 1:TT]
                v.tensor_copy(xnl2[:, :, :],
                              bass.AP(ln.tensor, ln.offset,
                                      [list(ln.ap[0]), list(ln.ap[1]), [0, 2]])
                              ).then_inc(sem["VBX"], 1)

            def v_C(i):
                ti = i % NTPS
                for d in range(CG):
                    gidx = CG * i + d
                    sl = gidx % 2
                    w(v, "T2", gidx + 1)
                    kv_d = big[:, 0, d, :]
                    st_d = big[:, 1, d, :]
                    v.tensor_copy(vsb[:, d, :], ps_v[sl][:, 0:TT])
                    v.tensor_mul(kv_d, ps_k[sl][:, 0:TT], vsb[:, d, :])
                    init = 0.0 if ti == 0 else carry[:, d:d + 1]
                    v.tensor_tensor_scan(st_d, _bcast_t(dec_sb[:, d:d + 1], TT),
                                         kv_d, init, OP.mult, OP.add)
                    v.scalar_tensor_tensor(kv_d, kv_d, col(I_TF, d),
                                           st_d, OP.mult, OP.add)
                    v.tensor_mul(obuf[:, d, :], ps_ra[sl][:, 0:TT], kv_d
                                 ).then_inc(sem["V3"], 1)
                v.tensor_copy(carry[:, :], _fl(big[:, 1, :, TT - 1:TT]))

            def v_D(i):
                w(v, "T3", i)
                v.tensor_mul(_fl(sqO[:, :, :]), _fl(obuf[:, :, :]), _fl(obuf[:, :, :])
                             ).then_inc(sem["VO"], 1)
                w(v, "T3A", i + 1)
                v.tensor_copy(sums[:, 0, :], ps_stat[:, 0:TT]).then_inc(sem["VE2"], 1)
                w(v, "T3", i + 1)
                v.tensor_copy(sums[:, 1, :], ps_stat[:, 0:TT]).then_inc(sem["S5"], 1)
                v.tensor_scalar_mul(wsp[:, 0, :], sums[:, 0, :], INV_C)
                v.tensor_scalar_mul(wsp[:, 1, :], sums[:, 1, :], INV_C)
                v.tensor_mul(wsp[:, 2, :], wsp[:, 0, :], wsp[:, 0, :])
                v.tensor_sub(wsp[:, 1, :], wsp[:, 1, :], wsp[:, 2, :])
                v.tensor_scalar_add(wsp[:, 2, :], wsp[:, 1, :], 1e-5)
                v.reciprocal(r13[:, 0, :], wsp[:, 2, :]).then_inc(sem["V4"], 1)
                w(v, "S6", i + 1)
                v.tensor_copy(a2b[:, :], inv13[:, 0, :])
                v.tensor_mul(wsp[:, 2, :], wsp[:, 0, :], inv13[:, 0, :])
                v.tensor_copy(b2mb[:, :], wsp[:, 2, :])
                v.tensor_mul(_fl(tmpb[:, :, :]), _fl(obuf[:, :, :]), bc(a2b))
                v.tensor_sub(_fl(tmpb[:, :, :]), _fl(tmpb[:, :, :]), bc(b2mb))
                for cg in range(CG):
                    mi = v.tensor_scalar(xln2[:, cg, :], tmpb[:, cg, :],
                                         col(I_LNG, cg), col(I_LNB, cg), OP.mult, OP.add)
                mi.then_inc(sem["V5"], 1)

            def v_E(i):
                w(v, "SP", 16 * (i - 1))
                for d in range(CG):
                    gidx = CG * i + d
                    sl = gidx % 2
                    w(v, "T4", gidx + 1)
                    v.tensor_add(partial[:, i % 2, d, :], ps_ra[sl][:, 0:TT],
                                 xT[:, i % 2, d, :]).then_inc(sem["V6"], 1)

            v_A(0)
            v_B(0)
            v_C(0)
            for i in range(NT - 1):
                v_A(i + 1)
                v_B(i + 1)
                v_D(i)
                v_E(i)
                v_C(i + 1)
            v_D(NT - 1)
            v_E(NT - 1)

            v.drain()
            v.tensor_reduce(mxacc[:, :], mxall[:, 0:NT], AX, OP.max)
            v.drain()
            v.sem_inc(sem["VG1"], 1)
            v.wait_ge(sem["G1"], 32)
            v.tensor_reduce(tiny[0:1, 3:4], mrow[0:1, :], AX, OP.max)
            v.drain()
            v.sem_inc(sem["VG2"], 1)
            v.wait_ge(sem["G2"], 16)
            v.tensor_scalar_mul(tiny[0:1, 5:6], tiny[0:1, 4:5], CTAU)
            v.tensor_scalar_mul(tiny[0:1, 6:7], tiny[0:1, 4:5], -CTAU)
            v.drain()
            v.sem_inc(sem["VT"], 1)
            v.wait_ge(sem["S7"], 1)
            w(v, "T1", NT)

            for i in range(NT):
                w(v, "L2", 32 * i + 16)
                v.tensor_scalar(_fl(xbf[:, :, :]), _fl(xn2b[:, i % 2, :, :]),
                                tauv[:, 1:2], tauv[:, 0:1], OP.max, OP.min
                                ).then_inc(sem["V8"], 1)
                w(v, "L2", 32 * (i + 1))
                w(v, "SO", 16 * (i - 1))
                for u in range(CG):
                    gidx = CG * i + u
                    sl = gidx % 2
                    w(v, "T5", gidx + 1)
                    v.scalar_tensor_tensor(big[:, i % 2, u, :], ps_ra[sl][:, 0:TT],
                                           thrb[:, 1:2], partial[:, i % 2, u, :],
                                           OP.mult, OP.add).then_inc(sem["V9"], 1)
    return nc


def _cvt(vec):
    return np.ascontiguousarray(
        np.asarray(vec, np.float32).reshape(-1)[:C].reshape(CG, P).T)


def kernel(L0_attn_x, L0_attn_time_mix_k, L0_attn_time_mix_v, L0_attn_time_mix_r,
           L0_attn_time_decay, L0_attn_time_first, L0_attn_Wk, L0_attn_Wv,
           L0_attn_Wr, L0_attn_Wo, L0_attn_ln_g, L0_attn_ln_b, L0_ffn_weight,
           L0_ffn_ln_gamma, L0_ffn_ln_beta):
    x = np.asarray(L0_attn_x, np.float32)
    consts = np.ascontiguousarray(np.stack([
        _cvt(L0_attn_time_mix_k), _cvt(L0_attn_time_mix_v), _cvt(L0_attn_time_mix_r),
        _cvt(L0_attn_time_decay), _cvt(L0_attn_time_first),
        _cvt(L0_attn_ln_g), _cvt(L0_attn_ln_b),
        _cvt(L0_ffn_ln_gamma), _cvt(L0_ffn_ln_beta)]))
    wkt = np.ascontiguousarray(np.asarray(L0_attn_Wk, np.float32).T)
    wvt = np.ascontiguousarray(np.asarray(L0_attn_Wv, np.float32).T)
    wrt = np.ascontiguousarray(np.asarray(L0_attn_Wr, np.float32).T)
    wot = np.ascontiguousarray(np.asarray(L0_attn_Wo, np.float32).T)
    wffn = np.ascontiguousarray(np.asarray(L0_ffn_weight, np.float32))

    nc = build_nc()
    in_maps = []
    for ci in range(NCORES):
        xs = np.ascontiguousarray(x[NSEQ * ci:NSEQ * (ci + 1)].transpose(0, 2, 1))
        in_maps.append({"x": xs, "wkt": wkt, "wvt": wvt, "wrt": wrt, "wot": wot,
                        "wffn": wffn, "consts": consts})
    res = run_bass_kernel_spmd(nc, in_maps, core_ids=list(range(NCORES)))
    out = np.empty((NCORES * NSEQ, TSEQ, C), np.float32)
    for ci in range(NCORES):
        out[NSEQ * ci:NSEQ * (ci + 1)] = res.results[ci]["out"].transpose(0, 2, 1)
    return out


# version 15
# speedup vs baseline: 1.2074x; 1.2074x over previous
import sys
if "/opt/trn_rl_repo" not in sys.path:
    sys.path.insert(0, "/opt/trn_rl_repo")

from contextlib import ExitStack

import numpy as np

import concourse.bass as bass
import concourse.mybir as mybir
from concourse.bass_utils import run_bass_kernel_spmd

F32 = mybir.dt.float32
BF16 = mybir.dt.bfloat16
AF = mybir.ActivationFunctionType
OP = mybir.AluOpType
AX = mybir.AxisListType.X

P = 128
CG = 8
TT = 256
NSEQ = 2
TSEQ = 4096
NTPS = TSEQ // TT
NT = NSEQ * NTPS
C = 1024
NCORES = 8

INV_C = 1.0 / 1024.0
CTAU = (128.0 - 1e-5) / 128.0
I_TMK, I_TMV, I_TMR, I_TD, I_TF, I_LNG, I_LNB, I_FG, I_FB = range(9)


def _ap(handle, off, dims):
    a = handle.ap()
    return bass.AP(a.tensor, off, [list(d) for d in dims])


def _fl(ap3):
    return ap3.rearrange("p a b -> p (a b)")


def _bcast_t(col_ap, n):
    return bass.AP(col_ap.tensor, col_ap.offset, [list(col_ap.ap[0]), [0, n]])


def build_nc():
    nc = bass.Bass("TRN2", target_bir_lowering=False, detect_race_conditions=False)

    x_d = nc.dram_tensor("x", [NSEQ, C, TSEQ], F32, kind="ExternalInput")
    wkt_d = nc.dram_tensor("wkt", [C, C], F32, kind="ExternalInput")
    wvt_d = nc.dram_tensor("wvt", [C, C], F32, kind="ExternalInput")
    wrt_d = nc.dram_tensor("wrt", [C, C], F32, kind="ExternalInput")
    wot_d = nc.dram_tensor("wot", [C, C], F32, kind="ExternalInput")
    wffn_d = nc.dram_tensor("wffn", [C, C], F32, kind="ExternalInput")
    co_d = nc.dram_tensor("consts", [9, P, CG], F32, kind="ExternalInput")
    out_d = nc.dram_tensor("out", [NSEQ, C, TSEQ], F32, kind="ExternalOutput")

    sc_part = nc.dram_tensor("sc_part", [NT, P, CG * TT], F32)
    sc_xn2 = nc.dram_tensor("sc_xn2", [NT, P, CG * TT], BF16)
    colbuf = nc.dram_tensor("colbuf", [P], F32)
    cc_in = nc.dram_tensor("cc_in", [1], F32)
    cc_out = nc.dram_tensor("cc_out", [1], F32)

    with ExitStack() as es:
        e = es.enter_context
        wk_sb = e(nc.sbuf_tensor("wk_sb", [P, CG, C], BF16))
        wv_sb = e(nc.sbuf_tensor("wv_sb", [P, CG, C], BF16))
        wr_sb = e(nc.sbuf_tensor("wr_sb", [P, CG, C], BF16))
        wo_sb = e(nc.sbuf_tensor("wo_sb", [P, CG, C], BF16))
        wq_sb = e(nc.sbuf_tensor("wq_sb", [P, CG, C], BF16))
        stage = e(nc.sbuf_tensor("stage", [P, C], F32))
        xT = e(nc.sbuf_tensor("xT", [P, 2, CG, TT], F32))
        xbf = e(nc.sbuf_tensor("xbf", [P, CG, TT], BF16))
        xn2b = e(nc.sbuf_tensor("xn2b", [P, 2, CG, TT], BF16))
        partial = e(nc.sbuf_tensor("partial", [P, 2, CG, TT], F32))
        big = e(nc.sbuf_tensor("big", [P, 2, CG, TT], F32))
        xn = e(nc.sbuf_tensor("xn", [P, CG, TT], BF16))
        tmpb = e(nc.sbuf_tensor("tmpb", [P, CG, TT], BF16))
        xxs = e(nc.sbuf_tensor("xxs", [P, CG, TT + 2], BF16))
        xk = e(nc.sbuf_tensor("xk", [P, CG, TT], BF16))
        xv = e(nc.sbuf_tensor("xv", [P, CG, TT], BF16))
        xr = e(nc.sbuf_tensor("xr", [P, CG, TT], BF16))
        obuf = e(nc.sbuf_tensor("obuf", [P, CG, TT], BF16))
        vsb = e(nc.sbuf_tensor("vsb", [P, CG, TT], BF16))
        sqA = e(nc.sbuf_tensor("sqA", [P, CG, TT], BF16))
        sqO = e(nc.sbuf_tensor("sqO", [P, CG, TT], BF16))
        xln2 = e(nc.sbuf_tensor("xln2", [P, CG, TT], BF16))
        sums = e(nc.sbuf_tensor("sums", [P, 2, TT], F32))
        wsp = e(nc.sbuf_tensor("wsp", [P, 4, TT], F32))
        r13 = e(nc.sbuf_tensor("r13", [P, 2, TT], F32))
        inv13 = e(nc.sbuf_tensor("inv13", [P, 2, TT], F32))
        a1b = e(nc.sbuf_tensor("a1b", [P, TT], BF16))
        b1mb = e(nc.sbuf_tensor("b1mb", [P, TT], BF16))
        A3b = e(nc.sbuf_tensor("A3b", [P, TT], BF16))
        B3mb = e(nc.sbuf_tensor("B3mb", [P, TT], BF16))
        a2b = e(nc.sbuf_tensor("a2b", [P, TT], BF16))
        b2mb = e(nc.sbuf_tensor("b2mb", [P, TT], BF16))
        co = e(nc.sbuf_tensor("co", [P, 9, CG], F32))
        dec_sb = e(nc.sbuf_tensor("dec_sb", [P, CG], F32))
        ones128 = e(nc.sbuf_tensor("ones128", [P, P], BF16))
        onesrow = e(nc.sbuf_tensor("onesrow", [1, P], F32))
        carry = e(nc.sbuf_tensor("carry", [P, CG], F32))
        xnl2 = e(nc.sbuf_tensor("xnl2", [P, CG, 2], BF16))
        zb2 = e(nc.sbuf_tensor("zb2", [P, CG, 2], BF16))
        mxacc = e(nc.sbuf_tensor("mxacc", [P, 1], F32))
        mxall = e(nc.sbuf_tensor("mxall", [P, NT], F32))
        absrow = e(nc.sbuf_tensor("absrow", [P, 1], F32))
        absall = e(nc.sbuf_tensor("absall", [P, CG], F32))
        thrb = e(nc.sbuf_tensor("thrb", [P, 2], F32))
        thrneg = e(nc.sbuf_tensor("thrneg", [P, 1], F32))
        tauv = e(nc.sbuf_tensor("tauv", [P, 2], F32))
        mrow = e(nc.sbuf_tensor("mrow", [1, P], F32))
        tiny = e(nc.sbuf_tensor("tiny", [1, 8], F32))

        ps_stat = e(nc.psum_tensor("ps_stat", [P, 512], F32))
        ps_k0 = e(nc.psum_tensor("ps_k0", [P, 512], F32))
        ps_k1 = e(nc.psum_tensor("ps_k1", [P, 512], F32))
        ps_v0 = e(nc.psum_tensor("ps_v0", [P, 512], F32))
        ps_v1 = e(nc.psum_tensor("ps_v1", [P, 512], F32))
        ps_ra0 = e(nc.psum_tensor("ps_ra0", [P, 512], F32))
        ps_ra1 = e(nc.psum_tensor("ps_ra1", [P, 512], F32))
        ps_small = e(nc.psum_tensor("ps_small", [P, 512], F32))
        ps_k = [ps_k0, ps_k1]
        ps_v = [ps_v0, ps_v1]
        ps_ra = [ps_ra0, ps_ra1]

        names = ("X W WG B1 L2 SX SP SO G1 G2 CC BG "
                 "S1 S2 S3 S5 S6 S7 S8 SB SD T1 T1A T2 T3 T3A T4 T5 T6 TB VE1 VE2 "
                 "V0 V1 V2 V3 V4 V5 V6 V7 V8 V9 VO VBX VG1 VG2 VT B2 B4 B5 B6").split()
        sem = {n: e(nc.semaphore(n)) for n in names}
        block = e(nc.Block())

        def w(eng, name, val):
            if val > 0:
                eng.wait_ge(sem[name], val)

        def col(k, cg):
            return co[:, k, cg:cg + 1]

        def bc(tile):
            a = tile[:, :]
            return bass.AP(a.tensor, a.offset, [[a.ap[0][0], P], [0, CG], [1, TT]])

        def x_tile_ap(i):
            seq, ti = divmod(i, NTPS)
            off = seq * C * TSEQ + ti * TT
            return _ap(x_d, off, [[TSEQ, P], [P * TSEQ, CG], [1, TT]])

        def out_tile_ap(i):
            seq, ti = divmod(i, NTPS)
            off = seq * C * TSEQ + ti * TT
            return _ap(out_d, off, [[TSEQ, P], [P * TSEQ, CG], [1, TT]])

        def scr_ap(handle, i):
            return _ap(handle, i * P * CG * TT, [[CG * TT, P], [1, CG * TT]])

        def wchunk_ap(handle, c_):
            return _ap(handle, c_ * P * C, [[C, P], [1, C]])

        @block.sync
        def _(sync):
            sync.dma_start(out=co[:, :, :],
                           in_=_ap(co_d, 0, [[CG, P], [P * CG, 9], [1, CG]])
                           ).then_inc(sem["W"], 16)
            for i in range(NT):
                w(sync, "V6", 8 * (i - 1))
                sync.dma_start(out=xT[:, i % 2, :, :], in_=x_tile_ap(i)
                               ).then_inc(sem["X"], 16)
            sync.wait_ge(sem["SX"], 16 * NT)
            sync.wait_ge(sem["SP"], 16 * NT)
            for i in range(NT):
                w(sync, "V8", i - 1)
                sync.dma_start(out=_fl(xn2b[:, i % 2, :, :]), in_=scr_ap(sc_xn2, i)
                               ).then_inc(sem["L2"], 16)
                w(sync, "V9", 8 * (i - 1))
                sync.dma_start(out=_fl(partial[:, i % 2, :, :]), in_=scr_ap(sc_part, i)
                               ).then_inc(sem["L2"], 16)

        @block.gpsimd
        def _(g):
            for wsb, wd in ((wk_sb, wkt_d), (wv_sb, wvt_d), (wr_sb, wrt_d),
                            (wo_sb, wot_d)):
                g.dma_start(out=wsb[:, :, :],
                            in_=_ap(wd, 0, [[C, P], [P * C, CG], [1, C]])
                            ).then_inc(sem["WG"], 16)
            for c_ in range(CG):
                w(g, "B2", c_)
                g.dma_start(out=stage[:, :], in_=wchunk_ap(wffn_d, c_)
                            ).then_inc(sem["B1"], 16)
            g.wait_ge(sem["B2"], CG + 1)
            g.dma_start(out=colbuf.ap(), in_=absrow[:, 0:1]).then_inc(sem["BG"], 16)
            g.wait_ge(sem["BG"], 16)
            g.dma_start(out=mrow[0:1, :],
                        in_=_ap(colbuf, 0, [[P, 1], [1, P]])).then_inc(sem["BG"], 16)
            for c_ in range(CG):
                w(g, "B6", c_)
                g.dma_start(out=stage[:, :], in_=wchunk_ap(wffn_d, c_)
                            ).then_inc(sem["B1"], 16)
            for i in range(NT):
                w(g, "V7", i + 1)
                g.dma_start(out=scr_ap(sc_xn2, i), in_=_fl(xn2b[:, i % 2, :, :])
                            ).then_inc(sem["SX"], 16)
                w(g, "V6", 8 * (i + 1))
                g.dma_start(out=scr_ap(sc_part, i), in_=_fl(partial[:, i % 2, :, :])
                            ).then_inc(sem["SP"], 16)
            g.wait_ge(sem["VG1"], 1)
            g.dma_start(out=colbuf.ap(), in_=mxacc[:, 0:1]).then_inc(sem["G1"], 16)
            g.wait_ge(sem["G1"], 16)
            g.dma_start(out=mrow[0:1, :],
                        in_=_ap(colbuf, 0, [[P, 1], [1, P]])).then_inc(sem["G1"], 16)
            g.wait_ge(sem["VG2"], 1)
            g.dma_start(out=cc_in.ap(), in_=tiny[0:1, 3:4]).then_inc(sem["G1"], 16)
            g.wait_ge(sem["G1"], 48)
            g.collective_compute(
                "AllReduce", OP.max,
                replica_groups=[list(range(NCORES))],
                ins=[cc_in.ap()],
                outs=[cc_out.ap()],
            ).then_inc(sem["CC"], 1)
            g.wait_ge(sem["CC"], 1)
            g.dma_start(out=tiny[0:1, 4:5], in_=cc_out.ap()).then_inc(sem["G2"], 16)
            for i in range(NT):
                w(g, "V9", 8 * (i + 1))
                g.dma_start(out=out_tile_ap(i), in_=big[:, i % 2, :, :]
                            ).then_inc(sem["SO"], 16)
            g.wait_ge(sem["SO"], 16 * NT)

        @block.scalar
        def _(s):
            s.preload_act_table(AF.Exp)
            s.wait_ge(sem["W"], 16)
            s.activation(dec_sb[:, :], co[:, I_TD, :], AF.Exp
                         ).then_inc(sem["SD"], 1)
            s.preload_act_table(AF.Sqrt)
            s.wait_ge(sem["TB"], 1)
            s.copy(thrb[:, :], ps_small[:, 2:4]).then_inc(sem["SB"], 1)

            def s_cast(i):
                w(s, "X", 16 * (i + 1))
                w(s, "T1", i)
                w(s, "V7", i)
                s.copy(_fl(xbf[:, :, :]), _fl(xT[:, i % 2, :, :])
                       ).then_inc(sem["S1"], 1)

            def s_sqrt13(i):
                w(s, "V1", i + 1)
                s.activation(_fl(inv13[:, :, :]), _fl(r13[:, :, :]), AF.Sqrt
                             ).then_inc(sem["S3"], 1)

            def s_sqrt2(i):
                w(s, "V4", i + 1)
                s.activation(inv13[:, 0, :], r13[:, 0, :], AF.Sqrt
                             ).then_inc(sem["S6"], 1)

            s_cast(0)
            s_sqrt13(0)
            for i in range(NT - 1):
                s_cast(i + 1)
                s_sqrt13(i + 1)
                s_sqrt2(i)
            s_sqrt2(NT - 1)
            s.wait_ge(sem["T6"], 1)
            s.copy(tauv[:, :], ps_small[:, 0:2]).then_inc(sem["S7"], 1)

        @block.tensor
        def _(t):
            t.wait_ge(sem["B4"], 1)
            t.matmul(ps_small[:, 2:4], onesrow[0:1, :], tiny[0:1, 1:3],
                     start=True, stop=True).then_inc(sem["TB"], 1)
            t.wait_ge(sem["WG"], 64)

            def t_stats1(j):
                w(t, "S1", j + 1)
                w(t, "V0", j + 1)
                w(t, "S5", j - 1)
                for cg in range(CG):
                    mm = t.matmul(ps_stat[:, 0:TT], ones128[:, :], xbf[:, cg, :],
                                  start=(cg == 0), stop=(cg == CG - 1))
                mm.then_inc(sem["T1A"], 1)
                w(t, "VE1", j + 1)
                for cg in range(CG):
                    mm = t.matmul(ps_stat[:, 0:TT], ones128[:, :], sqA[:, cg, :],
                                  start=(cg == 0), stop=(cg == CG - 1))
                mm.then_inc(sem["T1"], 1)

            def t_proj(j):
                w(t, "V2", j + 1)
                for d in range(CG):
                    gidx = CG * j + d
                    sl = gidx % 2
                    w(t, "V3", gidx - 1)
                    if j >= 1:
                        w(t, "V6", 8 * j - 1 + sl)
                    for cg in range(CG):
                        t.matmul(ps_k[sl][:, 0:TT], wk_sb[:, cg, d * P:(d + 1) * P],
                                 xk[:, cg, :], start=(cg == 0), stop=(cg == CG - 1))
                    for cg in range(CG):
                        t.matmul(ps_v[sl][:, 0:TT], wv_sb[:, cg, d * P:(d + 1) * P],
                                 xv[:, cg, :], start=(cg == 0), stop=(cg == CG - 1))
                    for cg in range(CG):
                        mm = t.matmul(ps_ra[sl][:, 0:TT], wr_sb[:, cg, d * P:(d + 1) * P],
                                      xr[:, cg, :], start=(cg == 0), stop=(cg == CG - 1))
                    mm.then_inc(sem["T2"], 1)

            def t_stats2(i):
                w(t, "VO", i + 1)
                for cg in range(CG):
                    mm = t.matmul(ps_stat[:, 0:TT], ones128[:, :], obuf[:, cg, :],
                                  start=(cg == 0), stop=(cg == CG - 1))
                mm.then_inc(sem["T3A"], 1)
                w(t, "VE2", i + 1)
                for cg in range(CG):
                    mm = t.matmul(ps_stat[:, 0:TT], ones128[:, :], sqO[:, cg, :],
                                  start=(cg == 0), stop=(cg == CG - 1))
                mm.then_inc(sem["T3"], 1)

            def t_wo(i):
                w(t, "V5", i + 1)
                for d in range(CG):
                    gidx = CG * i + d
                    sl = gidx % 2
                    w(t, "V6", gidx - 1)
                    for cg in range(CG):
                        mm = t.matmul(ps_ra[sl][:, 0:TT], wo_sb[:, cg, d * P:(d + 1) * P],
                                      xln2[:, cg, :], start=(cg == 0), stop=(cg == CG - 1))
                    mm.then_inc(sem["T4"], 1)

            t_stats1(0)
            t_proj(0)
            for i in range(NT - 1):
                t_stats1(i + 1)
                t_stats2(i)
                t_wo(i)
                t_proj(i + 1)
            t_stats2(NT - 1)
            t_wo(NT - 1)
            t.wait_ge(sem["VT"], 1)
            t.matmul(ps_small[:, 0:2], onesrow[0:1, :], tiny[0:1, 5:7],
                     start=True, stop=True).then_inc(sem["T6"], 1)
            t.wait_ge(sem["V6"], 8 * NT)
            for i in range(NT):
                w(t, "V8", i + 1)
                for u in range(CG):
                    gidx = CG * i + u
                    sl = gidx % 2
                    w(t, "V9", gidx - 1)
                    for dcg in range(CG):
                        mm = t.matmul(ps_ra[sl][:, 0:TT], wq_sb[:, dcg, u * P:(u + 1) * P],
                                      xbf[:, dcg, :], start=(dcg == 0), stop=(dcg == CG - 1))
                    mm.then_inc(sem["T5"], 1)

        @block.vector
        def _(v):
            v.memset(ones128[:, :], 1.0)
            v.memset(onesrow[0:1, :], 1.0)
            v.memset(zb2[:, :, :], 0.0)
            v.memset(mxall[:, :], 0.0)
            for c_ in range(CG):
                w(v, "B1", 16 * (c_ + 1))
                v.tensor_reduce(absall[:, c_:c_ + 1], stage[:, :], AX, OP.add,
                                apply_absolute_value=True).then_inc(sem["B2"], 1)
            v.drain()
            v.tensor_reduce(absrow[:, :], absall[:, :], AX, OP.add)
            v.drain()
            v.sem_inc(sem["B2"], 1)
            v.wait_ge(sem["BG"], 32)
            v.tensor_reduce(tiny[0:1, 0:1], mrow[0:1, :], AX, OP.add)
            v.drain()
            v.tensor_scalar_mul(tiny[0:1, 7:8], tiny[0:1, 0:1], 0.5 / 1048576.0)
            v.tensor_scalar_mul(tiny[0:1, 2:3], tiny[0:1, 0:1], 1.0 / 1048576.0)
            v.drain()
            v.tensor_scalar_add(tiny[0:1, 1:2], tiny[0:1, 7:8], 5e-6)
            v.drain()
            v.sem_inc(sem["B4"], 1)
            v.wait_ge(sem["SB"], 1)
            v.tensor_scalar_mul(thrneg[:, :], thrb[:, 0:1], -1.0)
            v.drain()
            for c_ in range(CG):
                w(v, "B1", 16 * (CG + c_ + 1))
                gv = xk[:, 0:4, :].rearrange("p a b -> p (a b)")
                lv = xk[:, 4:8, :].rearrange("p a b -> p (a b)")
                v.tensor_single_scalar(gv, stage[:, :], thrb[:, 0:1], OP.is_gt)
                v.tensor_single_scalar(lv, stage[:, :], thrneg[:, 0:1], OP.is_lt)
                v.tensor_sub(wq_sb[:, c_, :], gv, lv).then_inc(sem["B6"], 1)
            v.wait_ge(sem["W"], 16)
            v.wait_ge(sem["SD"], 1)

            def v_A(i):
                w(v, "S1", i + 1)
                w(v, "T1", i)
                v.tensor_mul(_fl(sqA[:, :, :]), _fl(xbf[:, :, :]), _fl(xbf[:, :, :])
                             ).then_inc(sem["V0"], 1)
                w(v, "T1A", i + 1)
                v.tensor_copy(sums[:, 0, :], ps_stat[:, 0:TT]).then_inc(sem["VE1"], 1)
                w(v, "T1", i + 1)
                v.tensor_copy(sums[:, 1, :], ps_stat[:, 0:TT]).then_inc(sem["S2"], 1)
                v.tensor_scalar_mul(wsp[:, 0, :], sums[:, 0, :], INV_C)
                v.tensor_scalar_mul(wsp[:, 1, :], sums[:, 1, :], INV_C)
                v.tensor_mul(wsp[:, 2, :], wsp[:, 0, :], wsp[:, 0, :])
                v.tensor_sub(wsp[:, 1, :], wsp[:, 1, :], wsp[:, 2, :])
                v.tensor_scalar_add(wsp[:, 2, :], wsp[:, 1, :], 1e-5)
                v.reciprocal(r13[:, 0, :], wsp[:, 2, :])
                v.tensor_mul(wsp[:, 3, :], wsp[:, 1, :], r13[:, 0, :])
                v.tensor_scalar_add(wsp[:, 3, :], wsp[:, 3, :], 1e-3)
                v.reciprocal(r13[:, 1, :], wsp[:, 3, :]).then_inc(sem["V1"], 1)
                w(v, "S3", i + 1)
                v.tensor_copy(a1b[:, :], inv13[:, 0, :])
                v.tensor_mul(wsp[:, 2, :], wsp[:, 0, :], inv13[:, 0, :])
                v.tensor_copy(b1mb[:, :], wsp[:, 2, :])
                v.tensor_mul(A3b[:, :], inv13[:, 0, :], inv13[:, 1, :])
                v.tensor_mul(B3mb[:, :], wsp[:, 2, :], inv13[:, 1, :])

            def v_B(i):
                ti = i % NTPS
                v.tensor_mul(_fl(tmpb[:, :, :]), _fl(xbf[:, :, :]), bc(a1b))
                v.tensor_sub(_fl(xn[:, :, :]), _fl(tmpb[:, :, :]), bc(b1mb))
                w(v, "SX", 16 * (i - 1))
                v.tensor_mul(_fl(tmpb[:, :, :]), _fl(xbf[:, :, :]), bc(A3b))
                v.tensor_sub(_fl(tmpb[:, :, :]), _fl(tmpb[:, :, :]), bc(B3mb))
                for cg in range(CG):
                    mi = v.tensor_scalar(xn2b[:, i % 2, cg, :], tmpb[:, cg, :],
                                         col(I_FG, cg), col(I_FB, cg), OP.mult, OP.add)
                mi.then_inc(sem["V7"], 1)
                v.tensor_reduce(mxall[:, i:i + 1], _fl(xn2b[:, i % 2, :, :]), AX,
                                OP.max, apply_absolute_value=True)
                bsrc = zb2 if ti == 0 else xnl2
                for cg in range(CG):
                    v.tensor_copy(xxs[:, cg, 0:2], bsrc[:, cg, :])
                v.drain()
                for cg in range(CG):
                    v.tensor_copy(xxs[:, cg, 2:TT + 2], xn[:, cg, 0:TT])
                ln = xn[:, :, TT - 1:TT]
                v.tensor_copy(xnl2[:, :, :],
                              bass.AP(ln.tensor, ln.offset,
                                      [list(ln.ap[0]), list(ln.ap[1]), [0, 2]]))
                w(v, "T1", i + 1)
                for cg in range(CG):
                    v.tensor_sub(sqA[:, cg, :], xn[:, cg, :], xxs[:, cg, 1:TT + 1])
                for kcol, buf in ((I_TMK, xk), (I_TMV, xv), (I_TMR, xr)):
                    for cg in range(CG):
                        mi = v.scalar_tensor_tensor(buf[:, cg, :], sqA[:, cg, :],
                                                    col(kcol, cg), xxs[:, cg, 1:TT + 1],
                                                    OP.mult, OP.add)
                mi.then_inc(sem["V2"], 1)

            def v_C(i):
                ti = i % NTPS
                for d in range(CG):
                    gidx = CG * i + d
                    sl = gidx % 2
                    w(v, "T2", gidx + 1)
                    kv_d = big[:, 0, d, :]
                    st_d = big[:, 1, d, :]
                    v.tensor_copy(vsb[:, d, :], ps_v[sl][:, 0:TT])
                    v.tensor_mul(kv_d, ps_k[sl][:, 0:TT], vsb[:, d, :])
                    init = 0.0 if ti == 0 else carry[:, d:d + 1]
                    v.tensor_tensor_scan(st_d, _bcast_t(dec_sb[:, d:d + 1], TT),
                                         kv_d, init, OP.mult, OP.add)
                    v.scalar_tensor_tensor(kv_d, kv_d, col(I_TF, d),
                                           st_d, OP.mult, OP.add)
                    v.tensor_mul(obuf[:, d, :], ps_ra[sl][:, 0:TT], kv_d
                                 ).then_inc(sem["V3"], 1)
                v.tensor_copy(carry[:, :], _fl(big[:, 1, :, TT - 1:TT]))

            def v_D(i):
                w(v, "T3", i)
                v.tensor_mul(_fl(sqO[:, :, :]), _fl(obuf[:, :, :]), _fl(obuf[:, :, :])
                             ).then_inc(sem["VO"], 1)
                w(v, "T3A", i + 1)
                v.tensor_copy(sums[:, 0, :], ps_stat[:, 0:TT]).then_inc(sem["VE2"], 1)
                w(v, "T3", i + 1)
                v.tensor_copy(sums[:, 1, :], ps_stat[:, 0:TT]).then_inc(sem["S5"], 1)
                v.tensor_scalar_mul(wsp[:, 0, :], sums[:, 0, :], INV_C)
                v.tensor_scalar_mul(wsp[:, 1, :], sums[:, 1, :], INV_C)
                v.tensor_mul(wsp[:, 2, :], wsp[:, 0, :], wsp[:, 0, :])
                v.tensor_sub(wsp[:, 1, :], wsp[:, 1, :], wsp[:, 2, :])
                v.tensor_scalar_add(wsp[:, 2, :], wsp[:, 1, :], 1e-5)
                v.reciprocal(r13[:, 0, :], wsp[:, 2, :]).then_inc(sem["V4"], 1)
                w(v, "S6", i + 1)
                v.tensor_copy(a2b[:, :], inv13[:, 0, :])
                v.tensor_mul(wsp[:, 2, :], wsp[:, 0, :], inv13[:, 0, :])
                v.tensor_copy(b2mb[:, :], wsp[:, 2, :])
                v.tensor_mul(_fl(tmpb[:, :, :]), _fl(obuf[:, :, :]), bc(a2b))
                v.tensor_sub(_fl(tmpb[:, :, :]), _fl(tmpb[:, :, :]), bc(b2mb))
                for cg in range(CG):
                    mi = v.tensor_scalar(xln2[:, cg, :], tmpb[:, cg, :],
                                         col(I_LNG, cg), col(I_LNB, cg), OP.mult, OP.add)
                mi.then_inc(sem["V5"], 1)

            def v_E(i):
                w(v, "SP", 16 * (i - 1))
                for d in range(CG):
                    gidx = CG * i + d
                    sl = gidx % 2
                    w(v, "T4", gidx + 1)
                    v.tensor_add(partial[:, i % 2, d, :], ps_ra[sl][:, 0:TT],
                                 xT[:, i % 2, d, :]).then_inc(sem["V6"], 1)

            v_A(0)
            v_B(0)
            v_C(0)
            for i in range(NT - 1):
                v_A(i + 1)
                v_B(i + 1)
                v_D(i)
                v_E(i)
                v_C(i + 1)
            v_D(NT - 1)
            v_E(NT - 1)

            v.drain()
            v.tensor_reduce(mxacc[:, :], mxall[:, 0:NT], AX, OP.max)
            v.drain()
            v.sem_inc(sem["VG1"], 1)
            v.wait_ge(sem["G1"], 32)
            v.tensor_reduce(tiny[0:1, 3:4], mrow[0:1, :], AX, OP.max)
            v.drain()
            v.sem_inc(sem["VG2"], 1)
            v.wait_ge(sem["G2"], 16)
            v.tensor_scalar_mul(tiny[0:1, 5:6], tiny[0:1, 4:5], CTAU)
            v.tensor_scalar_mul(tiny[0:1, 6:7], tiny[0:1, 4:5], -CTAU)
            v.drain()
            v.sem_inc(sem["VT"], 1)
            v.wait_ge(sem["S7"], 1)
            w(v, "T1", NT)

            for i in range(NT):
                w(v, "L2", 32 * i + 16)
                v.tensor_scalar(_fl(xbf[:, :, :]), _fl(xn2b[:, i % 2, :, :]),
                                tauv[:, 1:2], tauv[:, 0:1], OP.max, OP.min
                                ).then_inc(sem["V8"], 1)
                w(v, "L2", 32 * (i + 1))
                w(v, "SO", 16 * (i - 1))
                for u in range(CG):
                    gidx = CG * i + u
                    sl = gidx % 2
                    w(v, "T5", gidx + 1)
                    v.scalar_tensor_tensor(big[:, i % 2, u, :], ps_ra[sl][:, 0:TT],
                                           thrb[:, 1:2], partial[:, i % 2, u, :],
                                           OP.mult, OP.add).then_inc(sem["V9"], 1)
    return nc


def _cvt(vec):
    return np.ascontiguousarray(
        np.asarray(vec, np.float32).reshape(-1)[:C].reshape(CG, P).T)


def kernel(L0_attn_x, L0_attn_time_mix_k, L0_attn_time_mix_v, L0_attn_time_mix_r,
           L0_attn_time_decay, L0_attn_time_first, L0_attn_Wk, L0_attn_Wv,
           L0_attn_Wr, L0_attn_Wo, L0_attn_ln_g, L0_attn_ln_b, L0_ffn_weight,
           L0_ffn_ln_gamma, L0_ffn_ln_beta):
    x = np.asarray(L0_attn_x, np.float32)
    consts = np.ascontiguousarray(np.stack([
        _cvt(L0_attn_time_mix_k), _cvt(L0_attn_time_mix_v), _cvt(L0_attn_time_mix_r),
        _cvt(L0_attn_time_decay), _cvt(L0_attn_time_first),
        _cvt(L0_attn_ln_g), _cvt(L0_attn_ln_b),
        _cvt(L0_ffn_ln_gamma), _cvt(L0_ffn_ln_beta)]))
    wkt = np.ascontiguousarray(np.asarray(L0_attn_Wk, np.float32).T)
    wvt = np.ascontiguousarray(np.asarray(L0_attn_Wv, np.float32).T)
    wrt = np.ascontiguousarray(np.asarray(L0_attn_Wr, np.float32).T)
    wot = np.ascontiguousarray(np.asarray(L0_attn_Wo, np.float32).T)
    wffn = np.ascontiguousarray(np.asarray(L0_ffn_weight, np.float32))

    nc = build_nc()
    in_maps = []
    for ci in range(NCORES):
        xs = np.ascontiguousarray(x[NSEQ * ci:NSEQ * (ci + 1)].transpose(0, 2, 1))
        in_maps.append({"x": xs, "wkt": wkt, "wvt": wvt, "wrt": wrt, "wot": wot,
                        "wffn": wffn, "consts": consts})
    res = run_bass_kernel_spmd(nc, in_maps, core_ids=list(range(NCORES)))
    out = np.empty((NCORES * NSEQ, TSEQ, C), np.float32)
    for ci in range(NCORES):
        out[NSEQ * ci:NSEQ * (ci + 1)] = res.results[ci]["out"].transpose(0, 2, 1)
    return out


# version 16
# speedup vs baseline: 1.2109x; 1.0028x over previous
import sys
if "/opt/trn_rl_repo" not in sys.path:
    sys.path.insert(0, "/opt/trn_rl_repo")

from contextlib import ExitStack

import numpy as np

import concourse.bass as bass
import concourse.mybir as mybir
from concourse.bass_utils import run_bass_kernel_spmd

F32 = mybir.dt.float32
BF16 = mybir.dt.bfloat16
AF = mybir.ActivationFunctionType
OP = mybir.AluOpType
AX = mybir.AxisListType.X

P = 128
CG = 8
TT = 256
NSEQ = 2
TSEQ = 4096
NTPS = TSEQ // TT
NT = NSEQ * NTPS
C = 1024
NCORES = 8

INV_C = 1.0 / 1024.0
CTAU = (128.0 - 1e-5) / 128.0
I_TMK, I_TMV, I_TMR, I_TD, I_TF, I_LNG, I_LNB, I_FG, I_FB = range(9)


def _ap(handle, off, dims):
    a = handle.ap()
    return bass.AP(a.tensor, off, [list(d) for d in dims])


def _fl(ap3):
    return ap3.rearrange("p a b -> p (a b)")


def _bcast_t(col_ap, n):
    return bass.AP(col_ap.tensor, col_ap.offset, [list(col_ap.ap[0]), [0, n]])


def build_nc():
    nc = bass.Bass("TRN2", target_bir_lowering=False, detect_race_conditions=False)

    x_d = nc.dram_tensor("x", [NSEQ, C, TSEQ], F32, kind="ExternalInput")
    wkt_d = nc.dram_tensor("wkt", [C, C], F32, kind="ExternalInput")
    wvt_d = nc.dram_tensor("wvt", [C, C], F32, kind="ExternalInput")
    wrt_d = nc.dram_tensor("wrt", [C, C], F32, kind="ExternalInput")
    wot_d = nc.dram_tensor("wot", [C, C], F32, kind="ExternalInput")
    wffn_d = nc.dram_tensor("wffn", [C, C], F32, kind="ExternalInput")
    co_d = nc.dram_tensor("consts", [9, P, CG], F32, kind="ExternalInput")
    out_d = nc.dram_tensor("out", [NSEQ, C, TSEQ], F32, kind="ExternalOutput")

    sc_part = nc.dram_tensor("sc_part", [NT, P, CG * TT], F32)
    sc_xn2 = nc.dram_tensor("sc_xn2", [NT, P, CG * TT], BF16)
    colbuf = nc.dram_tensor("colbuf", [P], F32)
    cc_in = nc.dram_tensor("cc_in", [1], F32)
    cc_out = nc.dram_tensor("cc_out", [1], F32)

    with ExitStack() as es:
        e = es.enter_context
        wk_sb = e(nc.sbuf_tensor("wk_sb", [P, CG, C], BF16))
        wv_sb = e(nc.sbuf_tensor("wv_sb", [P, CG, C], BF16))
        wr_sb = e(nc.sbuf_tensor("wr_sb", [P, CG, C], BF16))
        wo_sb = e(nc.sbuf_tensor("wo_sb", [P, CG, C], BF16))
        wq_sb = e(nc.sbuf_tensor("wq_sb", [P, CG, C], BF16))
        stage = e(nc.sbuf_tensor("stage", [P, C], F32))
        xT = e(nc.sbuf_tensor("xT", [P, 2, CG, TT], F32))
        xs2 = e(nc.sbuf_tensor("xs2", [P, CG, 2, TT], BF16))
        xn2b = e(nc.sbuf_tensor("xn2b", [P, 2, CG, TT], BF16))
        partial = e(nc.sbuf_tensor("partial", [P, 2, CG, TT], F32))
        big = e(nc.sbuf_tensor("big", [P, 2, CG, TT], F32))
        xn = e(nc.sbuf_tensor("xn", [P, CG, TT], BF16))
        tmpb = e(nc.sbuf_tensor("tmpb", [P, CG, TT], BF16))
        xxs = e(nc.sbuf_tensor("xxs", [P, CG, TT + 2], BF16))
        xk = e(nc.sbuf_tensor("xk", [P, CG, TT], BF16))
        xv = e(nc.sbuf_tensor("xv", [P, CG, TT], BF16))
        xr = e(nc.sbuf_tensor("xr", [P, CG, TT], BF16))
        os2 = e(nc.sbuf_tensor("os2", [P, CG, 2, TT], BF16))
        vsb = e(nc.sbuf_tensor("vsb", [P, CG, TT], BF16))
        xln2 = e(nc.sbuf_tensor("xln2", [P, CG, TT], BF16))
        sums = e(nc.sbuf_tensor("sums", [P, 2, TT], F32))
        wsp = e(nc.sbuf_tensor("wsp", [P, 4, TT], F32))
        r13 = e(nc.sbuf_tensor("r13", [P, 2, TT], F32))
        inv13 = e(nc.sbuf_tensor("inv13", [P, 2, TT], F32))
        a1b = e(nc.sbuf_tensor("a1b", [P, TT], BF16))
        b1mb = e(nc.sbuf_tensor("b1mb", [P, TT], BF16))
        A3b = e(nc.sbuf_tensor("A3b", [P, TT], BF16))
        B3mb = e(nc.sbuf_tensor("B3mb", [P, TT], BF16))
        a2b = e(nc.sbuf_tensor("a2b", [P, TT], BF16))
        b2mb = e(nc.sbuf_tensor("b2mb", [P, TT], BF16))
        co = e(nc.sbuf_tensor("co", [P, 9, CG], F32))
        dec_sb = e(nc.sbuf_tensor("dec_sb", [P, CG], F32))
        ones128 = e(nc.sbuf_tensor("ones128", [P, P], BF16))
        onesrow = e(nc.sbuf_tensor("onesrow", [1, P], F32))
        carry = e(nc.sbuf_tensor("carry", [P, CG], F32))
        xnl2 = e(nc.sbuf_tensor("xnl2", [P, CG, 2], BF16))
        zb2 = e(nc.sbuf_tensor("zb2", [P, CG, 2], BF16))
        mxacc = e(nc.sbuf_tensor("mxacc", [P, 1], F32))
        mxall = e(nc.sbuf_tensor("mxall", [P, NT], F32))
        absrow = e(nc.sbuf_tensor("absrow", [P, 1], F32))
        absall = e(nc.sbuf_tensor("absall", [P, CG], F32))
        thrb = e(nc.sbuf_tensor("thrb", [P, 2], F32))
        thrneg = e(nc.sbuf_tensor("thrneg", [P, 1], F32))
        tauv = e(nc.sbuf_tensor("tauv", [P, 2], F32))
        mrow = e(nc.sbuf_tensor("mrow", [1, P], F32))
        tiny = e(nc.sbuf_tensor("tiny", [1, 8], F32))

        ps_stat = e(nc.psum_tensor("ps_stat", [P, 512], F32))
        ps_k0 = e(nc.psum_tensor("ps_k0", [P, 512], F32))
        ps_k1 = e(nc.psum_tensor("ps_k1", [P, 512], F32))
        ps_v0 = e(nc.psum_tensor("ps_v0", [P, 512], F32))
        ps_v1 = e(nc.psum_tensor("ps_v1", [P, 512], F32))
        ps_ra0 = e(nc.psum_tensor("ps_ra0", [P, 512], F32))
        ps_ra1 = e(nc.psum_tensor("ps_ra1", [P, 512], F32))
        ps_small = e(nc.psum_tensor("ps_small", [P, 512], F32))
        ps_k = [ps_k0, ps_k1]
        ps_v = [ps_v0, ps_v1]
        ps_ra = [ps_ra0, ps_ra1]

        names = ("X W WG B1 L2 SX SP SO G1 G2 CC BG "
                 "S1 S2 S3 S5 S6 S7 S8 SB SD T1 T1A T2 T3 T3A T4 T5 T6 TB VE1 VE2 "
                 "V0 V1 V2 V3 V4 V5 V6 V7 V8 V9 VO VBX VG1 VG2 VT B2 B4 B5 B6").split()
        sem = {n: e(nc.semaphore(n)) for n in names}
        block = e(nc.Block())

        def w(eng, name, val):
            if val > 0:
                eng.wait_ge(sem[name], val)

        def col(k, cg):
            return co[:, k, cg:cg + 1]

        def bc(tile):
            a = tile[:, :]
            return bass.AP(a.tensor, a.offset, [[a.ap[0][0], P], [0, CG], [1, TT]])

        def x_tile_ap(i):
            seq, ti = divmod(i, NTPS)
            off = seq * C * TSEQ + ti * TT
            return _ap(x_d, off, [[TSEQ, P], [P * TSEQ, CG], [1, TT]])

        def out_tile_ap(i):
            seq, ti = divmod(i, NTPS)
            off = seq * C * TSEQ + ti * TT
            return _ap(out_d, off, [[TSEQ, P], [P * TSEQ, CG], [1, TT]])

        def scr_ap(handle, i):
            return _ap(handle, i * P * CG * TT, [[CG * TT, P], [1, CG * TT]])

        def wchunk_ap(handle, c_):
            return _ap(handle, c_ * P * C, [[C, P], [1, C]])

        @block.sync
        def _(sync):
            sync.dma_start(out=co[:, :, :],
                           in_=_ap(co_d, 0, [[CG, P], [P * CG, 9], [1, CG]])
                           ).then_inc(sem["W"], 16)
            for i in range(NT):
                w(sync, "V6", 8 * (i - 1))
                sync.dma_start(out=xT[:, i % 2, :, :], in_=x_tile_ap(i)
                               ).then_inc(sem["X"], 16)
            sync.wait_ge(sem["SX"], 16 * NT)
            sync.wait_ge(sem["SP"], 16 * NT)
            for i in range(NT):
                w(sync, "V8", i - 1)
                sync.dma_start(out=_fl(xn2b[:, i % 2, :, :]), in_=scr_ap(sc_xn2, i)
                               ).then_inc(sem["L2"], 16)
                w(sync, "V9", 8 * (i - 1))
                sync.dma_start(out=_fl(partial[:, i % 2, :, :]), in_=scr_ap(sc_part, i)
                               ).then_inc(sem["L2"], 16)

        @block.gpsimd
        def _(g):
            for wsb, wd in ((wk_sb, wkt_d), (wv_sb, wvt_d), (wr_sb, wrt_d),
                            (wo_sb, wot_d)):
                g.dma_start(out=wsb[:, :, :],
                            in_=_ap(wd, 0, [[C, P], [P * C, CG], [1, C]])
                            ).then_inc(sem["WG"], 16)
            for c_ in range(CG):
                w(g, "B2", c_)
                g.dma_start(out=stage[:, :], in_=wchunk_ap(wffn_d, c_)
                            ).then_inc(sem["B1"], 16)
            g.wait_ge(sem["B2"], CG + 1)
            g.dma_start(out=colbuf.ap(), in_=absrow[:, 0:1]).then_inc(sem["BG"], 16)
            g.wait_ge(sem["BG"], 16)
            g.dma_start(out=mrow[0:1, :],
                        in_=_ap(colbuf, 0, [[P, 1], [1, P]])).then_inc(sem["BG"], 16)
            for c_ in range(CG):
                w(g, "B6", c_)
                g.dma_start(out=stage[:, :], in_=wchunk_ap(wffn_d, c_)
                            ).then_inc(sem["B1"], 16)
            for i in range(NT):
                w(g, "V7", i + 1)
                g.dma_start(out=scr_ap(sc_xn2, i), in_=_fl(xn2b[:, i % 2, :, :])
                            ).then_inc(sem["SX"], 16)
                w(g, "V6", 8 * (i + 1))
                g.dma_start(out=scr_ap(sc_part, i), in_=_fl(partial[:, i % 2, :, :])
                            ).then_inc(sem["SP"], 16)
            g.wait_ge(sem["VG1"], 1)
            g.dma_start(out=colbuf.ap(), in_=mxacc[:, 0:1]).then_inc(sem["G1"], 16)
            g.wait_ge(sem["G1"], 16)
            g.dma_start(out=mrow[0:1, :],
                        in_=_ap(colbuf, 0, [[P, 1], [1, P]])).then_inc(sem["G1"], 16)
            g.wait_ge(sem["VG2"], 1)
            g.dma_start(out=cc_in.ap(), in_=tiny[0:1, 3:4]).then_inc(sem["G1"], 16)
            g.wait_ge(sem["G1"], 48)
            g.collective_compute(
                "AllReduce", OP.max,
                replica_groups=[list(range(NCORES))],
                ins=[cc_in.ap()],
                outs=[cc_out.ap()],
            ).then_inc(sem["CC"], 1)
            g.wait_ge(sem["CC"], 1)
            g.dma_start(out=tiny[0:1, 4:5], in_=cc_out.ap()).then_inc(sem["G2"], 16)
            for i in range(NT):
                w(g, "V9", 8 * (i + 1))
                g.dma_start(out=out_tile_ap(i), in_=big[:, i % 2, :, :]
                            ).then_inc(sem["SO"], 16)
            g.wait_ge(sem["SO"], 16 * NT)

        @block.scalar
        def _(s):
            s.preload_act_table(AF.Exp)
            s.wait_ge(sem["W"], 16)
            s.activation(dec_sb[:, :], co[:, I_TD, :], AF.Exp
                         ).then_inc(sem["SD"], 1)
            s.preload_act_table(AF.Sqrt)
            s.wait_ge(sem["TB"], 1)
            s.copy(thrb[:, :], ps_small[:, 2:4]).then_inc(sem["SB"], 1)

            def s_cast(i):
                w(s, "X", 16 * (i + 1))
                w(s, "T1", i)
                w(s, "V7", i)
                s.copy(xs2[:, :, 0, :], xT[:, i % 2, :, :]
                       ).then_inc(sem["S1"], 1)

            def s_sqrt13(i):
                w(s, "V1", i + 1)
                s.activation(_fl(inv13[:, :, :]), _fl(r13[:, :, :]), AF.Sqrt
                             ).then_inc(sem["S3"], 1)

            def s_sqrt2(i):
                w(s, "V4", i + 1)
                s.activation(inv13[:, 0, :], r13[:, 0, :], AF.Sqrt
                             ).then_inc(sem["S6"], 1)

            s_cast(0)
            s_sqrt13(0)
            for i in range(NT - 1):
                s_cast(i + 1)
                s_sqrt13(i + 1)
                s_sqrt2(i)
            s_sqrt2(NT - 1)
            s.wait_ge(sem["T6"], 1)
            s.copy(tauv[:, :], ps_small[:, 0:2]).then_inc(sem["S7"], 1)

        @block.tensor
        def _(t):
            t.wait_ge(sem["B4"], 1)
            t.matmul(ps_small[:, 2:4], onesrow[0:1, :], tiny[0:1, 1:3],
                     start=True, stop=True).then_inc(sem["TB"], 1)
            t.wait_ge(sem["WG"], 64)

            def t_stats1(j):
                w(t, "S1", j + 1)
                w(t, "V0", j + 1)
                w(t, "S5", j - 1)
                for cg in range(CG):
                    mm = t.matmul(ps_stat[:, 0:512], ones128[:, :],
                                  xs2[:, cg, :, :].rearrange("p a b -> p (a b)"),
                                  start=(cg == 0), stop=(cg == CG - 1))
                mm.then_inc(sem["T1"], 1)

            def t_proj(j):
                w(t, "V2", j + 1)
                for d in range(CG):
                    gidx = CG * j + d
                    sl = gidx % 2
                    w(t, "V3", gidx - 1)
                    if j >= 1:
                        w(t, "V6", 8 * j - 1 + sl)
                    for cg in range(CG):
                        t.matmul(ps_k[sl][:, 0:TT], wk_sb[:, cg, d * P:(d + 1) * P],
                                 xk[:, cg, :], start=(cg == 0), stop=(cg == CG - 1))
                    for cg in range(CG):
                        t.matmul(ps_v[sl][:, 0:TT], wv_sb[:, cg, d * P:(d + 1) * P],
                                 xv[:, cg, :], start=(cg == 0), stop=(cg == CG - 1))
                    for cg in range(CG):
                        mm = t.matmul(ps_ra[sl][:, 0:TT], wr_sb[:, cg, d * P:(d + 1) * P],
                                      xr[:, cg, :], start=(cg == 0), stop=(cg == CG - 1))
                    mm.then_inc(sem["T2"], 1)

            def t_stats2(i):
                w(t, "VO", i + 1)
                w(t, "S2", min(i + 2, NT))
                for cg in range(CG):
                    mm = t.matmul(ps_stat[:, 0:512], ones128[:, :],
                                  os2[:, cg, :, :].rearrange("p a b -> p (a b)"),
                                  start=(cg == 0), stop=(cg == CG - 1))
                mm.then_inc(sem["T3"], 1)

            def t_wo(i):
                w(t, "V5", i + 1)
                for d in range(CG):
                    gidx = CG * i + d
                    sl = gidx % 2
                    w(t, "V6", gidx - 1)
                    for cg in range(CG):
                        mm = t.matmul(ps_ra[sl][:, 0:TT], wo_sb[:, cg, d * P:(d + 1) * P],
                                      xln2[:, cg, :], start=(cg == 0), stop=(cg == CG - 1))
                    mm.then_inc(sem["T4"], 1)

            t_stats1(0)
            t_proj(0)
            for i in range(NT - 1):
                t_stats1(i + 1)
                t_stats2(i)
                t_wo(i)
                t_proj(i + 1)
            t_stats2(NT - 1)
            t_wo(NT - 1)
            t.wait_ge(sem["VT"], 1)
            t.matmul(ps_small[:, 0:2], onesrow[0:1, :], tiny[0:1, 5:7],
                     start=True, stop=True).then_inc(sem["T6"], 1)
            t.wait_ge(sem["V6"], 8 * NT)
            for i in range(NT):
                w(t, "V8", i + 1)
                for u in range(CG):
                    gidx = CG * i + u
                    sl = gidx % 2
                    w(t, "V9", gidx - 1)
                    for dcg in range(CG):
                        mm = t.matmul(ps_ra[sl][:, 0:TT], wq_sb[:, dcg, u * P:(u + 1) * P],
                                      xs2[:, dcg, 0, :], start=(dcg == 0), stop=(dcg == CG - 1))
                    mm.then_inc(sem["T5"], 1)

        @block.vector
        def _(v):
            v.memset(ones128[:, :], 1.0)
            v.memset(onesrow[0:1, :], 1.0)
            v.memset(zb2[:, :, :], 0.0)
            v.memset(mxall[:, :], 0.0)
            for c_ in range(CG):
                w(v, "B1", 16 * (c_ + 1))
                v.tensor_reduce(absall[:, c_:c_ + 1], stage[:, :], AX, OP.add,
                                apply_absolute_value=True).then_inc(sem["B2"], 1)
            v.drain()
            v.tensor_reduce(absrow[:, :], absall[:, :], AX, OP.add)
            v.drain()
            v.sem_inc(sem["B2"], 1)
            v.wait_ge(sem["BG"], 32)
            v.tensor_reduce(tiny[0:1, 0:1], mrow[0:1, :], AX, OP.add)
            v.drain()
            v.tensor_scalar_mul(tiny[0:1, 7:8], tiny[0:1, 0:1], 0.5 / 1048576.0)
            v.tensor_scalar_mul(tiny[0:1, 2:3], tiny[0:1, 0:1], 1.0 / 1048576.0)
            v.drain()
            v.tensor_scalar_add(tiny[0:1, 1:2], tiny[0:1, 7:8], 5e-6)
            v.drain()
            v.sem_inc(sem["B4"], 1)
            v.wait_ge(sem["SB"], 1)
            v.tensor_scalar_mul(thrneg[:, :], thrb[:, 0:1], -1.0)
            v.drain()
            for c_ in range(CG):
                w(v, "B1", 16 * (CG + c_ + 1))
                gv = xk[:, 0:4, :].rearrange("p a b -> p (a b)")
                lv = xk[:, 4:8, :].rearrange("p a b -> p (a b)")
                v.tensor_single_scalar(gv, stage[:, :], thrb[:, 0:1], OP.is_gt)
                v.tensor_single_scalar(lv, stage[:, :], thrneg[:, 0:1], OP.is_lt)
                v.tensor_sub(wq_sb[:, c_, :], gv, lv).then_inc(sem["B6"], 1)
            v.wait_ge(sem["W"], 16)
            v.wait_ge(sem["SD"], 1)

            def v_A(i):
                w(v, "S1", i + 1)
                w(v, "T1", i)
                v.tensor_mul(xs2[:, :, 1, :], xs2[:, :, 0, :], xs2[:, :, 0, :]
                             ).then_inc(sem["V0"], 1)
                w(v, "T1", i + 1)
                v.tensor_copy(_fl(sums[:, :, :]), ps_stat[:, 0:512]
                              ).then_inc(sem["S2"], 1)
                v.tensor_scalar_mul(wsp[:, 0, :], sums[:, 0, :], INV_C)
                v.tensor_scalar_mul(wsp[:, 1, :], sums[:, 1, :], INV_C)
                v.tensor_mul(wsp[:, 2, :], wsp[:, 0, :], wsp[:, 0, :])
                v.tensor_sub(wsp[:, 1, :], wsp[:, 1, :], wsp[:, 2, :])
                v.tensor_scalar_add(wsp[:, 2, :], wsp[:, 1, :], 1e-5)
                v.reciprocal(r13[:, 0, :], wsp[:, 2, :])
                v.tensor_mul(wsp[:, 3, :], wsp[:, 1, :], r13[:, 0, :])
                v.tensor_scalar_add(wsp[:, 3, :], wsp[:, 3, :], 1e-3)
                v.reciprocal(r13[:, 1, :], wsp[:, 3, :]).then_inc(sem["V1"], 1)
                w(v, "S3", i + 1)
                v.tensor_copy(a1b[:, :], inv13[:, 0, :])
                v.tensor_mul(wsp[:, 2, :], wsp[:, 0, :], inv13[:, 0, :])
                v.tensor_copy(b1mb[:, :], wsp[:, 2, :])
                v.tensor_mul(A3b[:, :], inv13[:, 0, :], inv13[:, 1, :])
                v.tensor_mul(B3mb[:, :], wsp[:, 2, :], inv13[:, 1, :])

            def v_B(i):
                ti = i % NTPS
                v.tensor_mul(_fl(tmpb[:, :, :]), xs2[:, :, 0, :], bc(a1b))
                v.tensor_sub(_fl(xn[:, :, :]), _fl(tmpb[:, :, :]), bc(b1mb))
                w(v, "SX", 16 * (i - 1))
                v.tensor_mul(_fl(tmpb[:, :, :]), xs2[:, :, 0, :], bc(A3b))
                v.tensor_sub(_fl(tmpb[:, :, :]), _fl(tmpb[:, :, :]), bc(B3mb))
                for cg in range(CG):
                    mi = v.tensor_scalar(xn2b[:, i % 2, cg, :], tmpb[:, cg, :],
                                         col(I_FG, cg), col(I_FB, cg), OP.mult, OP.add)
                mi.then_inc(sem["V7"], 1)
                v.tensor_reduce(mxall[:, i:i + 1], _fl(xn2b[:, i % 2, :, :]), AX,
                                OP.max, apply_absolute_value=True)
                bsrc = zb2 if ti == 0 else xnl2
                for cg in range(CG):
                    v.tensor_copy(xxs[:, cg, 0:2], bsrc[:, cg, :])
                v.drain()
                for cg in range(CG):
                    v.tensor_copy(xxs[:, cg, 2:TT + 2], xn[:, cg, 0:TT])
                ln = xn[:, :, TT - 1:TT]
                v.tensor_copy(xnl2[:, :, :],
                              bass.AP(ln.tensor, ln.offset,
                                      [list(ln.ap[0]), list(ln.ap[1]), [0, 2]]))
                w(v, "T1", i + 1)
                for cg in range(CG):
                    v.tensor_sub(xs2[:, cg, 1, :], xn[:, cg, :], xxs[:, cg, 1:TT + 1])
                for kcol, buf in ((I_TMK, xk), (I_TMV, xv), (I_TMR, xr)):
                    for cg in range(CG):
                        mi = v.scalar_tensor_tensor(buf[:, cg, :], xs2[:, cg, 1, :],
                                                    col(kcol, cg), xxs[:, cg, 1:TT + 1],
                                                    OP.mult, OP.add)
                mi.then_inc(sem["V2"], 1)

            def v_C(i):
                ti = i % NTPS
                w(v, "T3", i)
                for d in range(CG):
                    gidx = CG * i + d
                    sl = gidx % 2
                    w(v, "T2", gidx + 1)
                    kv_d = big[:, 0, d, :]
                    st_d = big[:, 1, d, :]
                    v.tensor_copy(vsb[:, d, :], ps_v[sl][:, 0:TT])
                    v.tensor_mul(kv_d, ps_k[sl][:, 0:TT], vsb[:, d, :])
                    init = 0.0 if ti == 0 else carry[:, d:d + 1]
                    v.tensor_tensor_scan(st_d, _bcast_t(dec_sb[:, d:d + 1], TT),
                                         kv_d, init, OP.mult, OP.add)
                    v.scalar_tensor_tensor(kv_d, kv_d, col(I_TF, d),
                                           st_d, OP.mult, OP.add)
                    v.tensor_mul(os2[:, d, 0, :], ps_ra[sl][:, 0:TT], kv_d
                                 ).then_inc(sem["V3"], 1)
                v.tensor_copy(carry[:, :], _fl(big[:, 1, :, TT - 1:TT]))
                v.tensor_mul(os2[:, :, 1, :], os2[:, :, 0, :], os2[:, :, 0, :]
                             ).then_inc(sem["VO"], 1)

            def v_D(i):
                w(v, "T3", i + 1)
                v.tensor_copy(_fl(sums[:, :, :]), ps_stat[:, 0:512]
                              ).then_inc(sem["S5"], 1)
                v.tensor_scalar_mul(wsp[:, 0, :], sums[:, 0, :], INV_C)
                v.tensor_scalar_mul(wsp[:, 1, :], sums[:, 1, :], INV_C)
                v.tensor_mul(wsp[:, 2, :], wsp[:, 0, :], wsp[:, 0, :])
                v.tensor_sub(wsp[:, 1, :], wsp[:, 1, :], wsp[:, 2, :])
                v.tensor_scalar_add(wsp[:, 2, :], wsp[:, 1, :], 1e-5)
                v.reciprocal(r13[:, 0, :], wsp[:, 2, :]).then_inc(sem["V4"], 1)
                w(v, "S6", i + 1)
                v.tensor_copy(a2b[:, :], inv13[:, 0, :])
                v.tensor_mul(wsp[:, 2, :], wsp[:, 0, :], inv13[:, 0, :])
                v.tensor_copy(b2mb[:, :], wsp[:, 2, :])
                v.tensor_mul(_fl(tmpb[:, :, :]), os2[:, :, 0, :], bc(a2b))
                v.tensor_sub(_fl(tmpb[:, :, :]), _fl(tmpb[:, :, :]), bc(b2mb))
                for cg in range(CG):
                    mi = v.tensor_scalar(xln2[:, cg, :], tmpb[:, cg, :],
                                         col(I_LNG, cg), col(I_LNB, cg), OP.mult, OP.add)
                mi.then_inc(sem["V5"], 1)

            def v_E(i):
                w(v, "SP", 16 * (i - 1))
                for d in range(CG):
                    gidx = CG * i + d
                    sl = gidx % 2
                    w(v, "T4", gidx + 1)
                    v.tensor_add(partial[:, i % 2, d, :], ps_ra[sl][:, 0:TT],
                                 xT[:, i % 2, d, :]).then_inc(sem["V6"], 1)

            v_A(0)
            v_B(0)
            v_C(0)
            for i in range(NT - 1):
                v_A(i + 1)
                v_B(i + 1)
                v_D(i)
                v_E(i)
                v_C(i + 1)
            v_D(NT - 1)
            v_E(NT - 1)

            v.drain()
            v.tensor_reduce(mxacc[:, :], mxall[:, 0:NT], AX, OP.max)
            v.drain()
            v.sem_inc(sem["VG1"], 1)
            v.wait_ge(sem["G1"], 32)
            v.tensor_reduce(tiny[0:1, 3:4], mrow[0:1, :], AX, OP.max)
            v.drain()
            v.sem_inc(sem["VG2"], 1)
            v.wait_ge(sem["G2"], 16)
            v.tensor_scalar_mul(tiny[0:1, 5:6], tiny[0:1, 4:5], CTAU)
            v.tensor_scalar_mul(tiny[0:1, 6:7], tiny[0:1, 4:5], -CTAU)
            v.drain()
            v.sem_inc(sem["VT"], 1)
            v.wait_ge(sem["S7"], 1)
            w(v, "T1", NT)

            for i in range(NT):
                w(v, "L2", 32 * i + 16)
                v.tensor_scalar(xs2[:, :, 0, :], _fl(xn2b[:, i % 2, :, :]),
                                tauv[:, 1:2], tauv[:, 0:1], OP.max, OP.min
                                ).then_inc(sem["V8"], 1)
                w(v, "L2", 32 * (i + 1))
                w(v, "SO", 16 * (i - 1))
                for u in range(CG):
                    gidx = CG * i + u
                    sl = gidx % 2
                    w(v, "T5", gidx + 1)
                    v.scalar_tensor_tensor(big[:, i % 2, u, :], ps_ra[sl][:, 0:TT],
                                           thrb[:, 1:2], partial[:, i % 2, u, :],
                                           OP.mult, OP.add).then_inc(sem["V9"], 1)
    return nc


def _cvt(vec):
    return np.ascontiguousarray(
        np.asarray(vec, np.float32).reshape(-1)[:C].reshape(CG, P).T)


def kernel(L0_attn_x, L0_attn_time_mix_k, L0_attn_time_mix_v, L0_attn_time_mix_r,
           L0_attn_time_decay, L0_attn_time_first, L0_attn_Wk, L0_attn_Wv,
           L0_attn_Wr, L0_attn_Wo, L0_attn_ln_g, L0_attn_ln_b, L0_ffn_weight,
           L0_ffn_ln_gamma, L0_ffn_ln_beta):
    x = np.asarray(L0_attn_x, np.float32)
    consts = np.ascontiguousarray(np.stack([
        _cvt(L0_attn_time_mix_k), _cvt(L0_attn_time_mix_v), _cvt(L0_attn_time_mix_r),
        _cvt(L0_attn_time_decay), _cvt(L0_attn_time_first),
        _cvt(L0_attn_ln_g), _cvt(L0_attn_ln_b),
        _cvt(L0_ffn_ln_gamma), _cvt(L0_ffn_ln_beta)]))
    wkt = np.ascontiguousarray(np.asarray(L0_attn_Wk, np.float32).T)
    wvt = np.ascontiguousarray(np.asarray(L0_attn_Wv, np.float32).T)
    wrt = np.ascontiguousarray(np.asarray(L0_attn_Wr, np.float32).T)
    wot = np.ascontiguousarray(np.asarray(L0_attn_Wo, np.float32).T)
    wffn = np.ascontiguousarray(np.asarray(L0_ffn_weight, np.float32))

    nc = build_nc()
    in_maps = []
    for ci in range(NCORES):
        xs = np.ascontiguousarray(x[NSEQ * ci:NSEQ * (ci + 1)].transpose(0, 2, 1))
        in_maps.append({"x": xs, "wkt": wkt, "wvt": wvt, "wrt": wrt, "wot": wot,
                        "wffn": wffn, "consts": consts})
    res = run_bass_kernel_spmd(nc, in_maps, core_ids=list(range(NCORES)))
    out = np.empty((NCORES * NSEQ, TSEQ, C), np.float32)
    for ci in range(NCORES):
        out[NSEQ * ci:NSEQ * (ci + 1)] = res.results[ci]["out"].transpose(0, 2, 1)
    return out


# version 17
# speedup vs baseline: 1.2194x; 1.0071x over previous
import sys
if "/opt/trn_rl_repo" not in sys.path:
    sys.path.insert(0, "/opt/trn_rl_repo")

from contextlib import ExitStack

import numpy as np

import concourse.bass as bass
import concourse.mybir as mybir
from concourse.bass_utils import run_bass_kernel_spmd

F32 = mybir.dt.float32
BF16 = mybir.dt.bfloat16
AF = mybir.ActivationFunctionType
OP = mybir.AluOpType
AX = mybir.AxisListType.X

P = 128
CG = 8
TT = 256
NSEQ = 2
TSEQ = 4096
NTPS = TSEQ // TT
NT = NSEQ * NTPS
C = 1024
NCORES = 8

INV_C = 1.0 / 1024.0
CTAU = (128.0 - 1e-5) / 128.0
I_TMK, I_TMV, I_TMR, I_TD, I_TF, I_LNG, I_LNB, I_FG, I_FB = range(9)


def _ap(handle, off, dims):
    a = handle.ap()
    return bass.AP(a.tensor, off, [list(d) for d in dims])


def _fl(ap3):
    return ap3.rearrange("p a b -> p (a b)")


def _bcast_t(col_ap, n):
    return bass.AP(col_ap.tensor, col_ap.offset, [list(col_ap.ap[0]), [0, n]])


def build_nc():
    nc = bass.Bass("TRN2", target_bir_lowering=False, detect_race_conditions=False)

    x_d = nc.dram_tensor("x", [NSEQ, C, TSEQ], F32, kind="ExternalInput")
    wkt_d = nc.dram_tensor("wkt", [C, C], F32, kind="ExternalInput")
    wvt_d = nc.dram_tensor("wvt", [C, C], F32, kind="ExternalInput")
    wrt_d = nc.dram_tensor("wrt", [C, C], F32, kind="ExternalInput")
    wot_d = nc.dram_tensor("wot", [C, C], F32, kind="ExternalInput")
    wffn_d = nc.dram_tensor("wffn", [C, C], F32, kind="ExternalInput")
    co_d = nc.dram_tensor("consts", [9, P, CG], F32, kind="ExternalInput")
    out_d = nc.dram_tensor("out", [NSEQ, C, TSEQ], F32, kind="ExternalOutput")

    sc_part = nc.dram_tensor("sc_part", [NT, P, CG * TT], F32)
    sc_xn2 = nc.dram_tensor("sc_xn2", [NT, P, CG * TT], BF16)
    colbuf = nc.dram_tensor("colbuf", [P], F32)
    cc_in = nc.dram_tensor("cc_in", [1], F32)
    cc_out = nc.dram_tensor("cc_out", [1], F32)

    with ExitStack() as es:
        e = es.enter_context
        wk_sb = e(nc.sbuf_tensor("wk_sb", [P, CG, C], BF16))
        wv_sb = e(nc.sbuf_tensor("wv_sb", [P, CG, C], BF16))
        wr_sb = e(nc.sbuf_tensor("wr_sb", [P, CG, C], BF16))
        wo_sb = e(nc.sbuf_tensor("wo_sb", [P, CG, C], BF16))
        wq_sb = e(nc.sbuf_tensor("wq_sb", [P, CG, C], BF16))
        stage = e(nc.sbuf_tensor("stage", [P, C], F32))
        xT = e(nc.sbuf_tensor("xT", [P, 2, CG, TT], F32))
        xs2 = e(nc.sbuf_tensor("xs2", [P, CG, 2, TT], BF16))
        xn2b = e(nc.sbuf_tensor("xn2b", [P, 2, CG, TT], BF16))
        partial = e(nc.sbuf_tensor("partial", [P, 2, CG, TT], F32))
        big = e(nc.sbuf_tensor("big", [P, 2, CG, TT], F32))
        xn = e(nc.sbuf_tensor("xn", [P, CG, TT], BF16))
        tmpb = e(nc.sbuf_tensor("tmpb", [P, CG, TT], BF16))
        xxs = e(nc.sbuf_tensor("xxs", [P, CG, TT + 2], BF16))
        xk = e(nc.sbuf_tensor("xk", [P, CG, TT], BF16))
        xv = e(nc.sbuf_tensor("xv", [P, CG, TT], BF16))
        xr = e(nc.sbuf_tensor("xr", [P, CG, TT], BF16))
        os2 = e(nc.sbuf_tensor("os2", [P, CG, 2, TT], BF16))
        vsb = e(nc.sbuf_tensor("vsb", [P, CG, TT], BF16))
        xln2 = e(nc.sbuf_tensor("xln2", [P, CG, TT], BF16))
        sums = e(nc.sbuf_tensor("sums", [P, 2, TT], F32))
        wsp = e(nc.sbuf_tensor("wsp", [P, 4, TT], F32))
        r13 = e(nc.sbuf_tensor("r13", [P, 2, TT], F32))
        inv13 = e(nc.sbuf_tensor("inv13", [P, 2, TT], F32))
        a1b = e(nc.sbuf_tensor("a1b", [P, TT], BF16))
        b1mb = e(nc.sbuf_tensor("b1mb", [P, TT], BF16))
        A3b = e(nc.sbuf_tensor("A3b", [P, TT], BF16))
        B3mb = e(nc.sbuf_tensor("B3mb", [P, TT], BF16))
        a2b = e(nc.sbuf_tensor("a2b", [P, TT], BF16))
        b2mb = e(nc.sbuf_tensor("b2mb", [P, TT], BF16))
        co = e(nc.sbuf_tensor("co", [P, 9, CG], F32))
        dec_sb = e(nc.sbuf_tensor("dec_sb", [P, CG], F32))
        ones128 = e(nc.sbuf_tensor("ones128", [P, P], BF16))
        onesrow = e(nc.sbuf_tensor("onesrow", [1, P], F32))
        carry = e(nc.sbuf_tensor("carry", [P, CG], F32))
        xnl2 = e(nc.sbuf_tensor("xnl2", [P, CG, 2], BF16))
        zb2 = e(nc.sbuf_tensor("zb2", [P, CG, 2], BF16))
        mxacc = e(nc.sbuf_tensor("mxacc", [P, 1], F32))
        mxall = e(nc.sbuf_tensor("mxall", [P, NT], F32))
        absrow = e(nc.sbuf_tensor("absrow", [P, 1], F32))
        absall = e(nc.sbuf_tensor("absall", [P, CG], F32))
        thrb = e(nc.sbuf_tensor("thrb", [P, 2], F32))
        thrneg = e(nc.sbuf_tensor("thrneg", [P, 1], F32))
        tauv = e(nc.sbuf_tensor("tauv", [P, 2], F32))
        mrow = e(nc.sbuf_tensor("mrow", [1, P], F32))
        tiny = e(nc.sbuf_tensor("tiny", [1, 8], F32))

        ps_stat = e(nc.psum_tensor("ps_stat", [P, 512], F32))
        ps_k0 = e(nc.psum_tensor("ps_k0", [P, 512], F32))
        ps_k1 = e(nc.psum_tensor("ps_k1", [P, 512], F32))
        ps_v0 = e(nc.psum_tensor("ps_v0", [P, 512], F32))
        ps_v1 = e(nc.psum_tensor("ps_v1", [P, 512], F32))
        ps_ra0 = e(nc.psum_tensor("ps_ra0", [P, 512], F32))
        ps_ra1 = e(nc.psum_tensor("ps_ra1", [P, 512], F32))
        ps_small = e(nc.psum_tensor("ps_small", [P, 512], F32))
        ps_k = [ps_k0, ps_k1]
        ps_v = [ps_v0, ps_v1]
        ps_ra = [ps_ra0, ps_ra1]

        names = ("X W WG B1 L2 SX SP SO G1 G2 CC BG "
                 "S1 S2 S3 S5 S6 S7 S8 SB SD T1 T1A T2 T3 T3A T4 T5 T6 TB VE1 VE2 "
                 "V0 V1 V2 V3 V4 V5 V6 V7 V8 V9 VO VBX VG1 VG2 VT B2 B4 B5 B6").split()
        sem = {n: e(nc.semaphore(n)) for n in names}
        block = e(nc.Block())

        def w(eng, name, val):
            if val > 0:
                eng.wait_ge(sem[name], val)

        def col(k, cg):
            return co[:, k, cg:cg + 1]

        def bc(tile):
            a = tile[:, :]
            return bass.AP(a.tensor, a.offset, [[a.ap[0][0], P], [0, CG], [1, TT]])

        def x_tile_ap(i):
            seq, ti = divmod(i, NTPS)
            off = seq * C * TSEQ + ti * TT
            return _ap(x_d, off, [[TSEQ, P], [P * TSEQ, CG], [1, TT]])

        def out_tile_ap(i):
            seq, ti = divmod(i, NTPS)
            off = seq * C * TSEQ + ti * TT
            return _ap(out_d, off, [[TSEQ, P], [P * TSEQ, CG], [1, TT]])

        def scr_ap(handle, i):
            return _ap(handle, i * P * CG * TT, [[CG * TT, P], [1, CG * TT]])

        def wchunk_ap(handle, c_):
            return _ap(handle, c_ * P * C, [[C, P], [1, C]])

        @block.sync
        def _(sync):
            sync.dma_start(out=co[:, :, :],
                           in_=_ap(co_d, 0, [[CG, P], [P * CG, 9], [1, CG]])
                           ).then_inc(sem["W"], 16)
            for i in range(NT):
                w(sync, "V6", 8 * (i - 1))
                sync.dma_start(out=xT[:, i % 2, :, :], in_=x_tile_ap(i)
                               ).then_inc(sem["X"], 16)
            sync.wait_ge(sem["SX"], 16 * NT)
            sync.wait_ge(sem["SP"], 16 * NT)
            for i in range(NT):
                w(sync, "V8", i - 1)
                sync.dma_start(out=_fl(xn2b[:, i % 2, :, :]), in_=scr_ap(sc_xn2, i)
                               ).then_inc(sem["L2"], 16)
                w(sync, "V9", 8 * (i - 1))
                sync.dma_start(out=_fl(partial[:, i % 2, :, :]), in_=scr_ap(sc_part, i)
                               ).then_inc(sem["L2"], 16)

        @block.gpsimd
        def _(g):
            for wsb, wd in ((wk_sb, wkt_d), (wv_sb, wvt_d), (wr_sb, wrt_d),
                            (wo_sb, wot_d)):
                g.dma_start(out=wsb[:, :, :],
                            in_=_ap(wd, 0, [[C, P], [P * C, CG], [1, C]])
                            ).then_inc(sem["WG"], 16)
            for c_ in range(CG):
                w(g, "B2", c_)
                g.dma_start(out=stage[:, :], in_=wchunk_ap(wffn_d, c_)
                            ).then_inc(sem["B1"], 16)
            g.wait_ge(sem["B2"], CG + 1)
            g.dma_start(out=colbuf.ap(), in_=absrow[:, 0:1]).then_inc(sem["BG"], 16)
            g.wait_ge(sem["BG"], 16)
            g.dma_start(out=mrow[0:1, :],
                        in_=_ap(colbuf, 0, [[P, 1], [1, P]])).then_inc(sem["BG"], 16)
            for c_ in range(CG):
                w(g, "B6", c_)
                g.dma_start(out=stage[:, :], in_=wchunk_ap(wffn_d, c_)
                            ).then_inc(sem["B1"], 16)
            for i in range(NT):
                w(g, "V7", i + 1)
                g.dma_start(out=scr_ap(sc_xn2, i), in_=_fl(xn2b[:, i % 2, :, :])
                            ).then_inc(sem["SX"], 16)
                w(g, "V6", 8 * (i + 1))
                g.dma_start(out=scr_ap(sc_part, i), in_=_fl(partial[:, i % 2, :, :])
                            ).then_inc(sem["SP"], 16)
            g.wait_ge(sem["VG1"], 1)
            g.dma_start(out=colbuf.ap(), in_=mxacc[:, 0:1]).then_inc(sem["G1"], 16)
            g.wait_ge(sem["G1"], 16)
            g.dma_start(out=mrow[0:1, :],
                        in_=_ap(colbuf, 0, [[P, 1], [1, P]])).then_inc(sem["G1"], 16)
            g.wait_ge(sem["VG2"], 1)
            g.dma_start(out=cc_in.ap(), in_=tiny[0:1, 3:4]).then_inc(sem["G1"], 16)
            g.wait_ge(sem["G1"], 48)
            g.collective_compute(
                "AllReduce", OP.max,
                replica_groups=[list(range(NCORES))],
                ins=[cc_in.ap()],
                outs=[cc_out.ap()],
            ).then_inc(sem["CC"], 1)
            g.wait_ge(sem["CC"], 1)
            g.dma_start(out=tiny[0:1, 4:5], in_=cc_out.ap()).then_inc(sem["G2"], 16)
            for i in range(NT):
                w(g, "V9", 8 * (i + 1))
                g.dma_start(out=out_tile_ap(i), in_=big[:, i % 2, :, :]
                            ).then_inc(sem["SO"], 16)
            g.wait_ge(sem["SO"], 16 * NT)

        @block.scalar
        def _(s):
            s.preload_act_table(AF.Exp)
            s.wait_ge(sem["W"], 16)
            s.activation(dec_sb[:, :], co[:, I_TD, :], AF.Exp
                         ).then_inc(sem["SD"], 1)
            s.preload_act_table(AF.Sqrt)
            s.wait_ge(sem["TB"], 1)
            s.copy(thrb[:, :], ps_small[:, 2:4]).then_inc(sem["SB"], 1)

            def s_cast(i):
                w(s, "X", 16 * (i + 1))
                w(s, "T1", i)
                w(s, "V7", i)
                s.copy(xs2[:, :, 0, :], xT[:, i % 2, :, :]
                       ).then_inc(sem["S1"], 1)

            def s_sqrt13(i):
                w(s, "V1", i + 1)
                s.activation(_fl(inv13[:, :, :]), _fl(r13[:, :, :]), AF.Sqrt
                             ).then_inc(sem["S3"], 1)

            def s_sqrt2(i):
                w(s, "V4", i + 1)
                s.activation(inv13[:, 0, :], r13[:, 0, :], AF.Sqrt
                             ).then_inc(sem["S6"], 1)

            s_cast(0)
            s_sqrt13(0)
            for i in range(NT - 1):
                s_cast(i + 1)
                s_sqrt13(i + 1)
                s_sqrt2(i)
            s_sqrt2(NT - 1)
            s.wait_ge(sem["T6"], 1)
            s.copy(tauv[:, :], ps_small[:, 0:2]).then_inc(sem["S7"], 1)

        @block.tensor
        def _(t):
            t.wait_ge(sem["B4"], 1)
            t.matmul(ps_small[:, 2:4], onesrow[0:1, :], tiny[0:1, 1:3],
                     start=True, stop=True).then_inc(sem["TB"], 1)
            t.wait_ge(sem["WG"], 64)

            def t_stats1(j):
                w(t, "S1", j + 1)
                w(t, "V0", j + 1)
                w(t, "S5", j - 1)
                for cg in range(CG):
                    mm = t.matmul(ps_stat[:, 0:512], ones128[:, :],
                                  xs2[:, cg, :, :].rearrange("p a b -> p (a b)"),
                                  start=(cg == 0), stop=(cg == CG - 1))
                mm.then_inc(sem["T1"], 1)

            def t_proj(j):
                w(t, "V2", j + 1)
                for d in range(CG):
                    gidx = CG * j + d
                    sl = gidx % 2
                    w(t, "V3", gidx - 1)
                    if j >= 1:
                        w(t, "V6", 8 * j - 1 + sl)
                    for cg in range(CG):
                        t.matmul(ps_k[sl][:, 0:TT], wk_sb[:, cg, d * P:(d + 1) * P],
                                 xk[:, cg, :], start=(cg == 0), stop=(cg == CG - 1))
                    for cg in range(CG):
                        t.matmul(ps_v[sl][:, 0:TT], wv_sb[:, cg, d * P:(d + 1) * P],
                                 xv[:, cg, :], start=(cg == 0), stop=(cg == CG - 1))
                    for cg in range(CG):
                        mm = t.matmul(ps_ra[sl][:, 0:TT], wr_sb[:, cg, d * P:(d + 1) * P],
                                      xr[:, cg, :], start=(cg == 0), stop=(cg == CG - 1))
                    mm.then_inc(sem["T2"], 1)

            def t_stats2(i):
                w(t, "VO", i + 1)
                w(t, "S2", min(i + 2, NT))
                for cg in range(CG):
                    mm = t.matmul(ps_stat[:, 0:512], ones128[:, :],
                                  os2[:, cg, :, :].rearrange("p a b -> p (a b)"),
                                  start=(cg == 0), stop=(cg == CG - 1))
                mm.then_inc(sem["T3"], 1)

            def t_wo(i):
                w(t, "V5", i + 1)
                for d in range(CG):
                    gidx = CG * i + d
                    sl = gidx % 2
                    w(t, "V6", gidx - 1)
                    for cg in range(CG):
                        mm = t.matmul(ps_ra[sl][:, 0:TT], wo_sb[:, cg, d * P:(d + 1) * P],
                                      xln2[:, cg, :], start=(cg == 0), stop=(cg == CG - 1))
                    mm.then_inc(sem["T4"], 1)

            t_stats1(0)
            t_proj(0)
            for i in range(NT - 1):
                t_stats1(i + 1)
                t_stats2(i)
                t_wo(i)
                t_proj(i + 1)
            t_stats2(NT - 1)
            t_wo(NT - 1)
            t.wait_ge(sem["VT"], 1)
            t.matmul(ps_small[:, 0:2], onesrow[0:1, :], tiny[0:1, 5:7],
                     start=True, stop=True).then_inc(sem["T6"], 1)
            t.wait_ge(sem["V6"], 8 * NT)
            for i in range(NT):
                w(t, "V8", i + 1)
                for u in range(CG):
                    gidx = CG * i + u
                    sl = gidx % 2
                    w(t, "V9", gidx - 1)
                    for dcg in range(CG):
                        mm = t.matmul(ps_ra[sl][:, 0:TT], wq_sb[:, dcg, u * P:(u + 1) * P],
                                      xs2[:, dcg, 0, :], start=(dcg == 0), stop=(dcg == CG - 1))
                    mm.then_inc(sem["T5"], 1)

        @block.vector
        def _(v):
            v.memset(ones128[:, :], 1.0)
            v.memset(onesrow[0:1, :], 1.0)
            v.memset(zb2[:, :, :], 0.0)
            v.memset(mxall[:, :], 0.0)
            for c_ in range(CG):
                w(v, "B1", 16 * (c_ + 1))
                v.tensor_reduce(absall[:, c_:c_ + 1], stage[:, :], AX, OP.add,
                                apply_absolute_value=True).then_inc(sem["B2"], 1)
            v.drain()
            v.tensor_reduce(absrow[:, :], absall[:, :], AX, OP.add)
            v.drain()
            v.sem_inc(sem["B2"], 1)
            v.wait_ge(sem["BG"], 32)
            v.tensor_reduce(tiny[0:1, 0:1], mrow[0:1, :], AX, OP.add)
            v.drain()
            v.tensor_scalar_mul(tiny[0:1, 7:8], tiny[0:1, 0:1], 0.5 / 1048576.0)
            v.tensor_scalar_mul(tiny[0:1, 2:3], tiny[0:1, 0:1], 1.0 / 1048576.0)
            v.drain()
            v.tensor_scalar_add(tiny[0:1, 1:2], tiny[0:1, 7:8], 5e-6)
            v.drain()
            v.sem_inc(sem["B4"], 1)
            v.wait_ge(sem["SB"], 1)
            v.tensor_scalar_mul(thrneg[:, :], thrb[:, 0:1], -1.0)
            v.drain()
            for c_ in range(CG):
                w(v, "B1", 16 * (CG + c_ + 1))
                gv = xk[:, 0:4, :].rearrange("p a b -> p (a b)")
                lv = xk[:, 4:8, :].rearrange("p a b -> p (a b)")
                v.tensor_single_scalar(gv, stage[:, :], thrb[:, 0:1], OP.is_gt)
                v.tensor_single_scalar(lv, stage[:, :], thrneg[:, 0:1], OP.is_lt)
                v.tensor_sub(wq_sb[:, c_, :], gv, lv).then_inc(sem["B6"], 1)
            v.wait_ge(sem["W"], 16)
            v.wait_ge(sem["SD"], 1)

            def v_A(i):
                w(v, "S1", i + 1)
                w(v, "T1", i)
                v.tensor_mul(xs2[:, :, 1, :], xs2[:, :, 0, :], xs2[:, :, 0, :]
                             ).then_inc(sem["V0"], 1)
                w(v, "T1", i + 1)
                v.tensor_copy(_fl(sums[:, :, :]), ps_stat[:, 0:512]
                              ).then_inc(sem["S2"], 1)
                v.tensor_scalar_mul(wsp[:, 0, :], sums[:, 0, :], INV_C)
                v.tensor_scalar_mul(wsp[:, 1, :], sums[:, 1, :], INV_C)
                v.tensor_mul(wsp[:, 2, :], wsp[:, 0, :], wsp[:, 0, :])
                v.tensor_sub(wsp[:, 1, :], wsp[:, 1, :], wsp[:, 2, :])
                v.tensor_scalar_add(wsp[:, 2, :], wsp[:, 1, :], 1e-5)
                v.reciprocal(r13[:, 0, :], wsp[:, 2, :])
                v.tensor_mul(wsp[:, 3, :], wsp[:, 1, :], r13[:, 0, :])
                v.tensor_scalar_add(wsp[:, 3, :], wsp[:, 3, :], 1e-3)
                v.reciprocal(r13[:, 1, :], wsp[:, 3, :]).then_inc(sem["V1"], 1)
                w(v, "S3", i + 1)
                v.tensor_copy(a1b[:, :], inv13[:, 0, :])
                v.tensor_mul(wsp[:, 2, :], wsp[:, 0, :], inv13[:, 0, :])
                v.tensor_copy(b1mb[:, :], wsp[:, 2, :])
                v.tensor_mul(A3b[:, :], inv13[:, 0, :], inv13[:, 1, :])
                v.tensor_mul(B3mb[:, :], wsp[:, 2, :], inv13[:, 1, :])

            def v_B(i):
                ti = i % NTPS
                v.tensor_mul(_fl(tmpb[:, :, :]), xs2[:, :, 0, :], bc(a1b))
                v.tensor_sub(_fl(xn[:, :, :]), _fl(tmpb[:, :, :]), bc(b1mb))
                w(v, "SX", 16 * (i - 1))
                v.tensor_mul(_fl(tmpb[:, :, :]), xs2[:, :, 0, :], bc(A3b))
                v.tensor_sub(_fl(tmpb[:, :, :]), _fl(tmpb[:, :, :]), bc(B3mb))
                for cg in range(CG):
                    mi = v.tensor_scalar(xn2b[:, i % 2, cg, :], tmpb[:, cg, :],
                                         col(I_FG, cg), col(I_FB, cg), OP.mult, OP.add)
                mi.then_inc(sem["V7"], 1)
                v.tensor_reduce(mxall[:, i:i + 1], _fl(xn2b[:, i % 2, :, :]), AX,
                                OP.max, apply_absolute_value=True)
                bsrc = zb2 if ti == 0 else xnl2
                for cg in range(CG):
                    v.tensor_copy(xxs[:, cg, 0:2], bsrc[:, cg, :])
                v.drain()
                for cg in range(CG):
                    v.tensor_copy(xxs[:, cg, 2:TT + 2], xn[:, cg, 0:TT])
                ln = xn[:, :, TT - 1:TT]
                v.tensor_copy(xnl2[:, :, :],
                              bass.AP(ln.tensor, ln.offset,
                                      [list(ln.ap[0]), list(ln.ap[1]), [0, 2]]))
                w(v, "T1", i + 1)
                for cg in range(CG):
                    v.tensor_sub(xs2[:, cg, 1, :], xn[:, cg, :], xxs[:, cg, 1:TT + 1])
                for kcol, buf in ((I_TMK, xk), (I_TMV, xv), (I_TMR, xr)):
                    for cg in range(CG):
                        mi = v.scalar_tensor_tensor(buf[:, cg, :], xs2[:, cg, 1, :],
                                                    col(kcol, cg), xxs[:, cg, 1:TT + 1],
                                                    OP.mult, OP.add)
                mi.then_inc(sem["V2"], 1)

            def v_C(i):
                ti = i % NTPS
                w(v, "T3", i)
                for d in range(CG):
                    gidx = CG * i + d
                    sl = gidx % 2
                    w(v, "T2", gidx + 1)
                    kv_d = big[:, 0, d, :]
                    st_d = big[:, 1, d, :]
                    v.tensor_copy(vsb[:, d, :], ps_v[sl][:, 0:TT])
                    v.tensor_mul(kv_d, ps_k[sl][:, 0:TT], vsb[:, d, :])
                    init = 0.0 if ti == 0 else carry[:, d:d + 1]
                    v.tensor_tensor_scan(st_d, _bcast_t(dec_sb[:, d:d + 1], TT),
                                         kv_d, init, OP.mult, OP.add)
                    v.scalar_tensor_tensor(kv_d, kv_d, col(I_TF, d),
                                           st_d, OP.mult, OP.add)
                    v.tensor_mul(os2[:, d, 0, :], ps_ra[sl][:, 0:TT], kv_d
                                 ).then_inc(sem["V3"], 1)
                v.tensor_copy(carry[:, :], _fl(big[:, 1, :, TT - 1:TT]))
                v.tensor_mul(os2[:, :, 1, :], os2[:, :, 0, :], os2[:, :, 0, :]
                             ).then_inc(sem["VO"], 1)

            def v_D(i):
                w(v, "T3", i + 1)
                v.tensor_copy(_fl(sums[:, :, :]), ps_stat[:, 0:512]
                              ).then_inc(sem["S5"], 1)
                v.tensor_scalar_mul(wsp[:, 0, :], sums[:, 0, :], INV_C)
                v.tensor_scalar_mul(wsp[:, 1, :], sums[:, 1, :], INV_C)
                v.tensor_mul(wsp[:, 2, :], wsp[:, 0, :], wsp[:, 0, :])
                v.tensor_sub(wsp[:, 1, :], wsp[:, 1, :], wsp[:, 2, :])
                v.tensor_scalar_add(wsp[:, 2, :], wsp[:, 1, :], 1e-5)
                v.reciprocal(r13[:, 0, :], wsp[:, 2, :]).then_inc(sem["V4"], 1)
                w(v, "S6", i + 1)
                v.tensor_copy(a2b[:, :], inv13[:, 0, :])
                v.tensor_mul(wsp[:, 2, :], wsp[:, 0, :], inv13[:, 0, :])
                v.tensor_copy(b2mb[:, :], wsp[:, 2, :])
                v.tensor_mul(_fl(tmpb[:, :, :]), os2[:, :, 0, :], bc(a2b))
                v.tensor_sub(_fl(tmpb[:, :, :]), _fl(tmpb[:, :, :]), bc(b2mb))
                for cg in range(CG):
                    mi = v.tensor_scalar(xln2[:, cg, :], tmpb[:, cg, :],
                                         col(I_LNG, cg), col(I_LNB, cg), OP.mult, OP.add)
                mi.then_inc(sem["V5"], 1)

            def v_E(i):
                w(v, "SP", 16 * (i - 1))
                for d in range(CG):
                    gidx = CG * i + d
                    sl = gidx % 2
                    w(v, "T4", gidx + 1)
                    v.tensor_add(partial[:, i % 2, d, :], ps_ra[sl][:, 0:TT],
                                 xT[:, i % 2, d, :]).then_inc(sem["V6"], 1)

            def v_gamma_start():
                v.drain()
                v.tensor_reduce(mxacc[:, :], mxall[:, 0:NT], AX, OP.max)
                v.drain()
                v.sem_inc(sem["VG1"], 1)

            v_A(0)
            v_B(0)
            v_C(0)
            for i in range(NT - 1):
                v_A(i + 1)
                v_D(i)
                v_B(i + 1)
                if i == NT - 2:
                    v_gamma_start()
                v_E(i)
                v_C(i + 1)
            v_D(NT - 1)
            v_E(NT - 1)

            v.wait_ge(sem["G1"], 32)
            v.tensor_reduce(tiny[0:1, 3:4], mrow[0:1, :], AX, OP.max)
            v.drain()
            v.sem_inc(sem["VG2"], 1)
            v.wait_ge(sem["G2"], 16)
            v.tensor_scalar_mul(tiny[0:1, 5:6], tiny[0:1, 4:5], CTAU)
            v.tensor_scalar_mul(tiny[0:1, 6:7], tiny[0:1, 4:5], -CTAU)
            v.drain()
            v.sem_inc(sem["VT"], 1)
            v.wait_ge(sem["S7"], 1)
            w(v, "T1", NT)

            for i in range(NT):
                w(v, "L2", 32 * i + 16)
                v.tensor_scalar(xs2[:, :, 0, :], _fl(xn2b[:, i % 2, :, :]),
                                tauv[:, 1:2], tauv[:, 0:1], OP.max, OP.min
                                ).then_inc(sem["V8"], 1)
                w(v, "L2", 32 * (i + 1))
                w(v, "SO", 16 * (i - 1))
                for u in range(CG):
                    gidx = CG * i + u
                    sl = gidx % 2
                    w(v, "T5", gidx + 1)
                    v.scalar_tensor_tensor(big[:, i % 2, u, :], ps_ra[sl][:, 0:TT],
                                           thrb[:, 1:2], partial[:, i % 2, u, :],
                                           OP.mult, OP.add).then_inc(sem["V9"], 1)
    return nc


def _cvt(vec):
    return np.ascontiguousarray(
        np.asarray(vec, np.float32).reshape(-1)[:C].reshape(CG, P).T)


def kernel(L0_attn_x, L0_attn_time_mix_k, L0_attn_time_mix_v, L0_attn_time_mix_r,
           L0_attn_time_decay, L0_attn_time_first, L0_attn_Wk, L0_attn_Wv,
           L0_attn_Wr, L0_attn_Wo, L0_attn_ln_g, L0_attn_ln_b, L0_ffn_weight,
           L0_ffn_ln_gamma, L0_ffn_ln_beta):
    x = np.asarray(L0_attn_x, np.float32)
    consts = np.ascontiguousarray(np.stack([
        _cvt(L0_attn_time_mix_k), _cvt(L0_attn_time_mix_v), _cvt(L0_attn_time_mix_r),
        _cvt(L0_attn_time_decay), _cvt(L0_attn_time_first),
        _cvt(L0_attn_ln_g), _cvt(L0_attn_ln_b),
        _cvt(L0_ffn_ln_gamma), _cvt(L0_ffn_ln_beta)]))
    wkt = np.ascontiguousarray(np.asarray(L0_attn_Wk, np.float32).T)
    wvt = np.ascontiguousarray(np.asarray(L0_attn_Wv, np.float32).T)
    wrt = np.ascontiguousarray(np.asarray(L0_attn_Wr, np.float32).T)
    wot = np.ascontiguousarray(np.asarray(L0_attn_Wo, np.float32).T)
    wffn = np.ascontiguousarray(np.asarray(L0_ffn_weight, np.float32))

    nc = build_nc()
    in_maps = []
    for ci in range(NCORES):
        xs = np.ascontiguousarray(x[NSEQ * ci:NSEQ * (ci + 1)].transpose(0, 2, 1))
        in_maps.append({"x": xs, "wkt": wkt, "wvt": wvt, "wrt": wrt, "wot": wot,
                        "wffn": wffn, "consts": consts})
    res = run_bass_kernel_spmd(nc, in_maps, core_ids=list(range(NCORES)))
    out = np.empty((NCORES * NSEQ, TSEQ, C), np.float32)
    for ci in range(NCORES):
        out[NSEQ * ci:NSEQ * (ci + 1)] = res.results[ci]["out"].transpose(0, 2, 1)
    return out


# version 18
# speedup vs baseline: 1.2695x; 1.0411x over previous
import sys
if "/opt/trn_rl_repo" not in sys.path:
    sys.path.insert(0, "/opt/trn_rl_repo")

from contextlib import ExitStack

import numpy as np

import concourse.bass as bass
import concourse.mybir as mybir
from concourse.bass_utils import run_bass_kernel_spmd

F32 = mybir.dt.float32
BF16 = mybir.dt.bfloat16
AF = mybir.ActivationFunctionType
OP = mybir.AluOpType
AX = mybir.AxisListType.X

P = 128
CG = 8
TT = 256
NSEQ = 2
TSEQ = 4096
NTPS = TSEQ // TT
NT = NSEQ * NTPS
C = 1024
NCORES = 8

INV_C = 1.0 / 1024.0
CTAU = (128.0 - 1e-5) / 128.0
I_TMK, I_TMV, I_TMR, I_TD, I_TF, I_LNG, I_LNB, I_FG, I_FB = range(9)


def _ap(handle, off, dims):
    a = handle.ap()
    return bass.AP(a.tensor, off, [list(d) for d in dims])


def _fl(ap3):
    return ap3.rearrange("p a b -> p (a b)")


def _bcast_t(col_ap, n):
    return bass.AP(col_ap.tensor, col_ap.offset, [list(col_ap.ap[0]), [0, n]])


def build_nc():
    nc = bass.Bass("TRN2", target_bir_lowering=False, detect_race_conditions=False)

    x_d = nc.dram_tensor("x", [NSEQ, C, TSEQ], F32, kind="ExternalInput")
    wkt_d = nc.dram_tensor("wkt", [C, C], F32, kind="ExternalInput")
    wvt_d = nc.dram_tensor("wvt", [C, C], F32, kind="ExternalInput")
    wrt_d = nc.dram_tensor("wrt", [C, C], F32, kind="ExternalInput")
    wot_d = nc.dram_tensor("wot", [C, C], F32, kind="ExternalInput")
    wffn_d = nc.dram_tensor("wffn", [C, C], F32, kind="ExternalInput")
    co_d = nc.dram_tensor("consts", [9, P, CG], F32, kind="ExternalInput")
    out_d = nc.dram_tensor("out", [NSEQ, C, TSEQ], F32, kind="ExternalOutput")

    sc_part = nc.dram_tensor("sc_part", [NT, P, CG * TT], F32)
    sc_xn2 = nc.dram_tensor("sc_xn2", [NT, P, CG * TT], BF16)
    colbuf = nc.dram_tensor("colbuf", [P], F32)
    cc_in = nc.dram_tensor("cc_in", [1], F32)
    cc_out = nc.dram_tensor("cc_out", [1], F32)

    with ExitStack() as es:
        e = es.enter_context
        wk_sb = e(nc.sbuf_tensor("wk_sb", [P, CG, C], BF16))
        wv_sb = e(nc.sbuf_tensor("wv_sb", [P, CG, C], BF16))
        wr_sb = e(nc.sbuf_tensor("wr_sb", [P, CG, C], BF16))
        wo_sb = e(nc.sbuf_tensor("wo_sb", [P, CG, C], BF16))
        wq_sb = e(nc.sbuf_tensor("wq_sb", [P, CG, C], BF16))
        stage = e(nc.sbuf_tensor("stage", [P, C], F32))
        xT = e(nc.sbuf_tensor("xT", [P, 2, CG, TT], F32))
        xs2 = e(nc.sbuf_tensor("xs2", [P, CG, 2, TT], BF16))
        xn2b = e(nc.sbuf_tensor("xn2b", [P, 2, CG, TT], BF16))
        partial = e(nc.sbuf_tensor("partial", [P, 2, CG, TT], F32))
        big = e(nc.sbuf_tensor("big", [P, 2, CG, TT], F32))
        xn = e(nc.sbuf_tensor("xn", [P, CG, TT], BF16))
        tmpb = e(nc.sbuf_tensor("tmpb", [P, CG, TT], BF16))
        xxs = e(nc.sbuf_tensor("xxs", [P, CG, TT + 2], BF16))
        xk = e(nc.sbuf_tensor("xk", [P, CG, TT], BF16))
        xv = e(nc.sbuf_tensor("xv", [P, CG, TT], BF16))
        xr = e(nc.sbuf_tensor("xr", [P, CG, TT], BF16))
        os2 = e(nc.sbuf_tensor("os2", [P, CG, 2, TT], BF16))
        vsb = e(nc.sbuf_tensor("vsb", [P, CG, TT], BF16))
        xln2 = e(nc.sbuf_tensor("xln2", [P, CG, TT], BF16))
        sums = e(nc.sbuf_tensor("sums", [P, 2, TT], F32))
        wsp = e(nc.sbuf_tensor("wsp", [P, 4, TT], F32))
        r13 = e(nc.sbuf_tensor("r13", [P, 2, TT], F32))
        inv13 = e(nc.sbuf_tensor("inv13", [P, 2, TT], F32))
        a1b = e(nc.sbuf_tensor("a1b", [P, TT], BF16))
        b1mb = e(nc.sbuf_tensor("b1mb", [P, TT], BF16))
        A3b = e(nc.sbuf_tensor("A3b", [P, TT], BF16))
        B3mb = e(nc.sbuf_tensor("B3mb", [P, TT], BF16))
        a2b = e(nc.sbuf_tensor("a2b", [P, TT], BF16))
        b2mb = e(nc.sbuf_tensor("b2mb", [P, TT], BF16))
        co = e(nc.sbuf_tensor("co", [P, 9, CG], F32))
        dec_sb = e(nc.sbuf_tensor("dec_sb", [P, CG], F32))
        ones128 = e(nc.sbuf_tensor("ones128", [P, P], BF16))
        onesrow = e(nc.sbuf_tensor("onesrow", [1, P], F32))
        carry = e(nc.sbuf_tensor("carry", [P, CG], F32))
        xnl2 = e(nc.sbuf_tensor("xnl2", [P, CG, 2], BF16))
        zb2 = e(nc.sbuf_tensor("zb2", [P, CG, 2], BF16))
        mxacc = e(nc.sbuf_tensor("mxacc", [P, 1], F32))
        mxall = e(nc.sbuf_tensor("mxall", [P, NT], F32))
        absrow = e(nc.sbuf_tensor("absrow", [P, 1], F32))
        absall = e(nc.sbuf_tensor("absall", [P, CG], F32))
        thrb = e(nc.sbuf_tensor("thrb", [P, 2], F32))
        thrneg = e(nc.sbuf_tensor("thrneg", [P, 1], F32))
        tauv = e(nc.sbuf_tensor("tauv", [P, 2], F32))
        mrow = e(nc.sbuf_tensor("mrow", [1, P], F32))
        tiny = e(nc.sbuf_tensor("tiny", [1, 8], F32))

        ps_stat = e(nc.psum_tensor("ps_stat", [P, 512], F32))
        ps_k0 = e(nc.psum_tensor("ps_k0", [P, 512], F32))
        ps_k1 = e(nc.psum_tensor("ps_k1", [P, 512], F32))
        ps_v0 = e(nc.psum_tensor("ps_v0", [P, 512], F32))
        ps_v1 = e(nc.psum_tensor("ps_v1", [P, 512], F32))
        ps_ra0 = e(nc.psum_tensor("ps_ra0", [P, 512], F32))
        ps_ra1 = e(nc.psum_tensor("ps_ra1", [P, 512], F32))
        ps_small = e(nc.psum_tensor("ps_small", [P, 512], F32))
        ps_k = [ps_k0, ps_k1]
        ps_v = [ps_v0, ps_v1]
        ps_ra = [ps_ra0, ps_ra1]

        names = ("X W WG B1 L2 SX SP SO G1 G2 CC BG "
                 "S1 S2 S3 S5 S6 S7 S8 SB SD T1 T1A T2 T2V T2K T3 T3A T4 T5 T6 TB VE1 VE2 "
                 "V0 V1 V2 V3 V4 V5 V6 V7 V8 V9 VO VBX VG1 VG2 VT B2 B4 B5 B6").split()
        sem = {n: e(nc.semaphore(n)) for n in names}
        block = e(nc.Block())

        def w(eng, name, val):
            if val > 0:
                eng.wait_ge(sem[name], val)

        def col(k, cg):
            return co[:, k, cg:cg + 1]

        def bc(tile):
            a = tile[:, :]
            return bass.AP(a.tensor, a.offset, [[a.ap[0][0], P], [0, CG], [1, TT]])

        def x_tile_ap(i):
            seq, ti = divmod(i, NTPS)
            off = seq * C * TSEQ + ti * TT
            return _ap(x_d, off, [[TSEQ, P], [P * TSEQ, CG], [1, TT]])

        def out_tile_ap(i):
            seq, ti = divmod(i, NTPS)
            off = seq * C * TSEQ + ti * TT
            return _ap(out_d, off, [[TSEQ, P], [P * TSEQ, CG], [1, TT]])

        def scr_ap(handle, i):
            return _ap(handle, i * P * CG * TT, [[CG * TT, P], [1, CG * TT]])

        def wchunk_ap(handle, c_):
            return _ap(handle, c_ * P * C, [[C, P], [1, C]])

        @block.sync
        def _(sync):
            sync.dma_start(out=co[:, :, :],
                           in_=_ap(co_d, 0, [[CG, P], [P * CG, 9], [1, CG]])
                           ).then_inc(sem["W"], 16)
            for i in range(NT):
                w(sync, "V6", 8 * (i - 1))
                sync.dma_start(out=xT[:, i % 2, :, :], in_=x_tile_ap(i)
                               ).then_inc(sem["X"], 16)
            sync.wait_ge(sem["SX"], 16 * NT)
            sync.wait_ge(sem["SP"], 16 * NT)
            for i in range(NT):
                w(sync, "V8", i - 1)
                sync.dma_start(out=_fl(xn2b[:, i % 2, :, :]), in_=scr_ap(sc_xn2, i)
                               ).then_inc(sem["L2"], 16)
                w(sync, "V9", 8 * (i - 1))
                sync.dma_start(out=_fl(partial[:, i % 2, :, :]), in_=scr_ap(sc_part, i)
                               ).then_inc(sem["L2"], 16)

        @block.gpsimd
        def _(g):
            for wsb, wd in ((wk_sb, wkt_d), (wv_sb, wvt_d), (wr_sb, wrt_d),
                            (wo_sb, wot_d)):
                g.dma_start(out=wsb[:, :, :],
                            in_=_ap(wd, 0, [[C, P], [P * C, CG], [1, C]])
                            ).then_inc(sem["WG"], 16)
            for c_ in range(CG):
                w(g, "B2", c_)
                g.dma_start(out=stage[:, :], in_=wchunk_ap(wffn_d, c_)
                            ).then_inc(sem["B1"], 16)
            g.wait_ge(sem["B2"], CG + 1)
            g.dma_start(out=colbuf.ap(), in_=absrow[:, 0:1]).then_inc(sem["BG"], 16)
            g.wait_ge(sem["BG"], 16)
            g.dma_start(out=mrow[0:1, :],
                        in_=_ap(colbuf, 0, [[P, 1], [1, P]])).then_inc(sem["BG"], 16)
            for c_ in range(CG):
                w(g, "B6", c_)
                g.dma_start(out=stage[:, :], in_=wchunk_ap(wffn_d, c_)
                            ).then_inc(sem["B1"], 16)
            for i in range(NT):
                w(g, "V7", i + 1)
                g.dma_start(out=scr_ap(sc_xn2, i), in_=_fl(xn2b[:, i % 2, :, :])
                            ).then_inc(sem["SX"], 16)
                w(g, "V6", 8 * (i + 1))
                g.dma_start(out=scr_ap(sc_part, i), in_=_fl(partial[:, i % 2, :, :])
                            ).then_inc(sem["SP"], 16)
            g.wait_ge(sem["VG1"], 1)
            g.dma_start(out=colbuf.ap(), in_=mxacc[:, 0:1]).then_inc(sem["G1"], 16)
            g.wait_ge(sem["G1"], 16)
            g.dma_start(out=mrow[0:1, :],
                        in_=_ap(colbuf, 0, [[P, 1], [1, P]])).then_inc(sem["G1"], 16)
            g.wait_ge(sem["VG2"], 1)
            g.dma_start(out=cc_in.ap(), in_=tiny[0:1, 3:4]).then_inc(sem["G1"], 16)
            g.wait_ge(sem["G1"], 48)
            g.collective_compute(
                "AllReduce", OP.max,
                replica_groups=[list(range(NCORES))],
                ins=[cc_in.ap()],
                outs=[cc_out.ap()],
            ).then_inc(sem["CC"], 1)
            g.wait_ge(sem["CC"], 1)
            g.dma_start(out=tiny[0:1, 4:5], in_=cc_out.ap()).then_inc(sem["G2"], 16)
            for i in range(NT):
                w(g, "V9", 8 * (i + 1))
                g.dma_start(out=out_tile_ap(i), in_=big[:, i % 2, :, :]
                            ).then_inc(sem["SO"], 16)
            g.wait_ge(sem["SO"], 16 * NT)

        @block.scalar
        def _(s):
            s.preload_act_table(AF.Exp)
            s.wait_ge(sem["W"], 16)
            s.activation(dec_sb[:, :], co[:, I_TD, :], AF.Exp
                         ).then_inc(sem["SD"], 1)
            s.preload_act_table(AF.Sqrt)
            s.wait_ge(sem["TB"], 1)
            s.copy(thrb[:, :], ps_small[:, 2:4]).then_inc(sem["SB"], 1)

            def s_cast(i):
                w(s, "X", 16 * (i + 1))
                w(s, "T1", i)
                w(s, "V7", i)
                s.copy(xs2[:, :, 0, :], xT[:, i % 2, :, :]
                       ).then_inc(sem["S1"], 1)

            def s_sqrt13(i):
                w(s, "V1", i + 1)
                s.activation(_fl(inv13[:, :, :]), _fl(r13[:, :, :]), AF.Sqrt
                             ).then_inc(sem["S3"], 1)

            def s_sqrt2(i):
                w(s, "V4", i + 1)
                s.activation(inv13[:, 0, :], r13[:, 0, :], AF.Sqrt
                             ).then_inc(sem["S6"], 1)

            s_cast(0)
            s_sqrt13(0)
            for i in range(NT - 1):
                s_cast(i + 1)
                s_sqrt13(i + 1)
                s_sqrt2(i)
            s_sqrt2(NT - 1)
            s.wait_ge(sem["T6"], 1)
            s.copy(tauv[:, :], ps_small[:, 0:2]).then_inc(sem["S7"], 1)

        @block.tensor
        def _(t):
            t.wait_ge(sem["B4"], 1)
            t.matmul(ps_small[:, 2:4], onesrow[0:1, :], tiny[0:1, 1:3],
                     start=True, stop=True).then_inc(sem["TB"], 1)
            t.wait_ge(sem["WG"], 64)

            def t_stats1(j):
                w(t, "S1", j + 1)
                w(t, "V0", j + 1)
                w(t, "S5", j - 1)
                for cg in range(CG):
                    mm = t.matmul(ps_stat[:, 0:512], ones128[:, :],
                                  xs2[:, cg, :, :].rearrange("p a b -> p (a b)"),
                                  start=(cg == 0), stop=(cg == CG - 1))
                mm.then_inc(sem["T1"], 1)

            def t_proj(j):
                w(t, "V2", j + 1)
                for d in range(CG):
                    gidx = CG * j + d
                    sl = gidx % 2
                    w(t, "V3", gidx - 1)
                    if j >= 1:
                        w(t, "V6", 8 * j - 1 + sl)
                    for cg in range(CG):
                        mm = t.matmul(ps_v[sl][:, 0:TT], wv_sb[:, cg, d * P:(d + 1) * P],
                                      xv[:, cg, :], start=(cg == 0), stop=(cg == CG - 1))
                    mm.then_inc(sem["T2V"], 1)
                    for cg in range(CG):
                        mm = t.matmul(ps_k[sl][:, 0:TT], wk_sb[:, cg, d * P:(d + 1) * P],
                                 xk[:, cg, :], start=(cg == 0), stop=(cg == CG - 1))
                    mm.then_inc(sem["T2K"], 1)
                    for cg in range(CG):
                        mm = t.matmul(ps_ra[sl][:, 0:TT], wr_sb[:, cg, d * P:(d + 1) * P],
                                      xr[:, cg, :], start=(cg == 0), stop=(cg == CG - 1))
                    mm.then_inc(sem["T2"], 1)

            def t_stats2(i):
                w(t, "VO", i + 1)
                w(t, "S2", min(i + 2, NT))
                for cg in range(CG):
                    mm = t.matmul(ps_stat[:, 0:512], ones128[:, :],
                                  os2[:, cg, :, :].rearrange("p a b -> p (a b)"),
                                  start=(cg == 0), stop=(cg == CG - 1))
                mm.then_inc(sem["T3"], 1)

            def t_wo(i):
                w(t, "V5", i + 1)
                for d in range(CG):
                    gidx = CG * i + d
                    sl = gidx % 2
                    w(t, "V6", gidx - 1)
                    for cg in range(CG):
                        mm = t.matmul(ps_ra[sl][:, 0:TT], wo_sb[:, cg, d * P:(d + 1) * P],
                                      xln2[:, cg, :], start=(cg == 0), stop=(cg == CG - 1))
                    mm.then_inc(sem["T4"], 1)

            t_stats1(0)
            t_proj(0)
            for i in range(NT - 1):
                t_stats1(i + 1)
                t_stats2(i)
                t_wo(i)
                t_proj(i + 1)
            t_stats2(NT - 1)
            t_wo(NT - 1)
            t.wait_ge(sem["VT"], 1)
            t.matmul(ps_small[:, 0:2], onesrow[0:1, :], tiny[0:1, 5:7],
                     start=True, stop=True).then_inc(sem["T6"], 1)
            t.wait_ge(sem["V6"], 8 * NT)
            for i in range(NT):
                w(t, "V8", i + 1)
                for u in range(CG):
                    gidx = CG * i + u
                    sl = gidx % 2
                    w(t, "V9", gidx - 1)
                    for dcg in range(CG):
                        mm = t.matmul(ps_ra[sl][:, 0:TT], wq_sb[:, dcg, u * P:(u + 1) * P],
                                      xs2[:, dcg, 0, :], start=(dcg == 0), stop=(dcg == CG - 1))
                    mm.then_inc(sem["T5"], 1)

        @block.vector
        def _(v):
            v.memset(ones128[:, :], 1.0)
            v.memset(onesrow[0:1, :], 1.0)
            v.memset(zb2[:, :, :], 0.0)
            v.memset(mxall[:, :], 0.0)
            for c_ in range(CG):
                w(v, "B1", 16 * (c_ + 1))
                v.tensor_reduce(absall[:, c_:c_ + 1], stage[:, :], AX, OP.add,
                                apply_absolute_value=True).then_inc(sem["B2"], 1)
            v.drain()
            v.tensor_reduce(absrow[:, :], absall[:, :], AX, OP.add)
            v.drain()
            v.sem_inc(sem["B2"], 1)
            v.wait_ge(sem["BG"], 32)
            v.tensor_reduce(tiny[0:1, 0:1], mrow[0:1, :], AX, OP.add)
            v.drain()
            v.tensor_scalar_mul(tiny[0:1, 7:8], tiny[0:1, 0:1], 0.5 / 1048576.0)
            v.tensor_scalar_mul(tiny[0:1, 2:3], tiny[0:1, 0:1], 1.0 / 1048576.0)
            v.drain()
            v.tensor_scalar_add(tiny[0:1, 1:2], tiny[0:1, 7:8], 5e-6)
            v.drain()
            v.sem_inc(sem["B4"], 1)
            v.wait_ge(sem["SB"], 1)
            v.tensor_scalar_mul(thrneg[:, :], thrb[:, 0:1], -1.0)
            v.drain()
            for c_ in range(CG):
                w(v, "B1", 16 * (CG + c_ + 1))
                gv = xk[:, 0:4, :].rearrange("p a b -> p (a b)")
                lv = xk[:, 4:8, :].rearrange("p a b -> p (a b)")
                v.tensor_single_scalar(gv, stage[:, :], thrb[:, 0:1], OP.is_gt)
                v.tensor_single_scalar(lv, stage[:, :], thrneg[:, 0:1], OP.is_lt)
                v.tensor_sub(wq_sb[:, c_, :], gv, lv).then_inc(sem["B6"], 1)
            v.wait_ge(sem["W"], 16)
            v.wait_ge(sem["SD"], 1)

            def v_A(i):
                w(v, "S1", i + 1)
                w(v, "T1", i)
                v.tensor_mul(xs2[:, :, 1, :], xs2[:, :, 0, :], xs2[:, :, 0, :]
                             ).then_inc(sem["V0"], 1)
                w(v, "T1", i + 1)
                v.tensor_copy(_fl(sums[:, :, :]), ps_stat[:, 0:512]
                              ).then_inc(sem["S2"], 1)
                v.tensor_scalar_mul(wsp[:, 0, :], sums[:, 0, :], INV_C)
                v.tensor_scalar_mul(wsp[:, 1, :], sums[:, 1, :], INV_C)
                v.tensor_mul(wsp[:, 2, :], wsp[:, 0, :], wsp[:, 0, :])
                v.tensor_sub(wsp[:, 1, :], wsp[:, 1, :], wsp[:, 2, :])
                v.tensor_scalar_add(wsp[:, 2, :], wsp[:, 1, :], 1e-5)
                v.reciprocal(r13[:, 0, :], wsp[:, 2, :])
                v.tensor_mul(wsp[:, 3, :], wsp[:, 1, :], r13[:, 0, :])
                v.tensor_scalar_add(wsp[:, 3, :], wsp[:, 3, :], 1e-3)
                v.reciprocal(r13[:, 1, :], wsp[:, 3, :]).then_inc(sem["V1"], 1)
                w(v, "S3", i + 1)
                v.tensor_copy(a1b[:, :], inv13[:, 0, :])
                v.tensor_mul(wsp[:, 2, :], wsp[:, 0, :], inv13[:, 0, :])
                v.tensor_copy(b1mb[:, :], wsp[:, 2, :])
                v.tensor_mul(A3b[:, :], inv13[:, 0, :], inv13[:, 1, :])
                v.tensor_mul(B3mb[:, :], wsp[:, 2, :], inv13[:, 1, :])

            def v_B(i):
                ti = i % NTPS
                v.tensor_mul(_fl(tmpb[:, :, :]), xs2[:, :, 0, :], bc(a1b))
                v.tensor_sub(_fl(xn[:, :, :]), _fl(tmpb[:, :, :]), bc(b1mb))
                w(v, "SX", 16 * (i - 1))
                v.tensor_mul(_fl(tmpb[:, :, :]), xs2[:, :, 0, :], bc(A3b))
                v.tensor_sub(_fl(tmpb[:, :, :]), _fl(tmpb[:, :, :]), bc(B3mb))
                for cg in range(CG):
                    mi = v.tensor_scalar(xn2b[:, i % 2, cg, :], tmpb[:, cg, :],
                                         col(I_FG, cg), col(I_FB, cg), OP.mult, OP.add)
                mi.then_inc(sem["V7"], 1)
                v.tensor_reduce(mxall[:, i:i + 1], _fl(xn2b[:, i % 2, :, :]), AX,
                                OP.max, apply_absolute_value=True)
                bsrc = zb2 if ti == 0 else xnl2
                for cg in range(CG):
                    v.tensor_copy(xxs[:, cg, 0:2], bsrc[:, cg, :])
                v.drain()
                for cg in range(CG):
                    v.tensor_copy(xxs[:, cg, 2:TT + 2], xn[:, cg, 0:TT])
                ln = xn[:, :, TT - 1:TT]
                v.tensor_copy(xnl2[:, :, :],
                              bass.AP(ln.tensor, ln.offset,
                                      [list(ln.ap[0]), list(ln.ap[1]), [0, 2]]))
                w(v, "T1", i + 1)
                for cg in range(CG):
                    v.tensor_sub(xs2[:, cg, 1, :], xn[:, cg, :], xxs[:, cg, 1:TT + 1])
                for kcol, buf in ((I_TMK, xk), (I_TMV, xv), (I_TMR, xr)):
                    for cg in range(CG):
                        mi = v.scalar_tensor_tensor(buf[:, cg, :], xs2[:, cg, 1, :],
                                                    col(kcol, cg), xxs[:, cg, 1:TT + 1],
                                                    OP.mult, OP.add)
                mi.then_inc(sem["V2"], 1)

            def v_C(i):
                ti = i % NTPS
                w(v, "T3", i)
                for d in range(CG):
                    gidx = CG * i + d
                    sl = gidx % 2
                    w(v, "T2V", gidx + 1)
                    kv_d = big[:, 0, d, :]
                    st_d = big[:, 1, d, :]
                    v.tensor_copy(vsb[:, d, :], ps_v[sl][:, 0:TT])
                    w(v, "T2K", gidx + 1)
                    v.tensor_mul(kv_d, ps_k[sl][:, 0:TT], vsb[:, d, :])
                    init = 0.0 if ti == 0 else carry[:, d:d + 1]
                    v.tensor_tensor_scan(st_d, _bcast_t(dec_sb[:, d:d + 1], TT),
                                         kv_d, init, OP.mult, OP.add)
                    v.scalar_tensor_tensor(kv_d, kv_d, col(I_TF, d),
                                           st_d, OP.mult, OP.add)
                    w(v, "T2", gidx + 1)
                    v.tensor_mul(os2[:, d, 0, :], ps_ra[sl][:, 0:TT], kv_d
                                 ).then_inc(sem["V3"], 1)
                v.tensor_copy(carry[:, :], _fl(big[:, 1, :, TT - 1:TT]))
                v.tensor_mul(os2[:, :, 1, :], os2[:, :, 0, :], os2[:, :, 0, :]
                             ).then_inc(sem["VO"], 1)

            def v_D(i):
                w(v, "T3", i + 1)
                v.tensor_copy(_fl(sums[:, :, :]), ps_stat[:, 0:512]
                              ).then_inc(sem["S5"], 1)
                v.tensor_scalar_mul(wsp[:, 0, :], sums[:, 0, :], INV_C)
                v.tensor_scalar_mul(wsp[:, 1, :], sums[:, 1, :], INV_C)
                v.tensor_mul(wsp[:, 2, :], wsp[:, 0, :], wsp[:, 0, :])
                v.tensor_sub(wsp[:, 1, :], wsp[:, 1, :], wsp[:, 2, :])
                v.tensor_scalar_add(wsp[:, 2, :], wsp[:, 1, :], 1e-5)
                v.reciprocal(r13[:, 0, :], wsp[:, 2, :]).then_inc(sem["V4"], 1)
                w(v, "S6", i + 1)
                v.tensor_copy(a2b[:, :], inv13[:, 0, :])
                v.tensor_mul(wsp[:, 2, :], wsp[:, 0, :], inv13[:, 0, :])
                v.tensor_copy(b2mb[:, :], wsp[:, 2, :])
                v.tensor_mul(_fl(tmpb[:, :, :]), os2[:, :, 0, :], bc(a2b))
                v.tensor_sub(_fl(tmpb[:, :, :]), _fl(tmpb[:, :, :]), bc(b2mb))
                for cg in range(CG):
                    mi = v.tensor_scalar(xln2[:, cg, :], tmpb[:, cg, :],
                                         col(I_LNG, cg), col(I_LNB, cg), OP.mult, OP.add)
                mi.then_inc(sem["V5"], 1)

            def v_E(i):
                w(v, "SP", 16 * (i - 1))
                for d in range(CG):
                    gidx = CG * i + d
                    sl = gidx % 2
                    w(v, "T4", gidx + 1)
                    v.tensor_add(partial[:, i % 2, d, :], ps_ra[sl][:, 0:TT],
                                 xT[:, i % 2, d, :]).then_inc(sem["V6"], 1)

            def v_gamma_start():
                v.drain()
                v.tensor_reduce(mxacc[:, :], mxall[:, 0:NT], AX, OP.max)
                v.drain()
                v.sem_inc(sem["VG1"], 1)

            v_A(0)
            v_B(0)
            v_C(0)
            for i in range(NT - 1):
                v_A(i + 1)
                v_D(i)
                v_B(i + 1)
                if i == NT - 2:
                    v_gamma_start()
                v_E(i)
                v_C(i + 1)
            v_D(NT - 1)
            v_E(NT - 1)

            v.wait_ge(sem["G1"], 32)
            v.tensor_reduce(tiny[0:1, 3:4], mrow[0:1, :], AX, OP.max)
            v.drain()
            v.sem_inc(sem["VG2"], 1)
            v.wait_ge(sem["G2"], 16)
            v.tensor_scalar_mul(tiny[0:1, 5:6], tiny[0:1, 4:5], CTAU)
            v.tensor_scalar_mul(tiny[0:1, 6:7], tiny[0:1, 4:5], -CTAU)
            v.drain()
            v.sem_inc(sem["VT"], 1)
            v.wait_ge(sem["S7"], 1)
            w(v, "T1", NT)

            for i in range(NT):
                w(v, "L2", 32 * i + 16)
                v.tensor_scalar(xs2[:, :, 0, :], _fl(xn2b[:, i % 2, :, :]),
                                tauv[:, 1:2], tauv[:, 0:1], OP.max, OP.min
                                ).then_inc(sem["V8"], 1)
                w(v, "L2", 32 * (i + 1))
                w(v, "SO", 16 * (i - 1))
                for u in range(CG):
                    gidx = CG * i + u
                    sl = gidx % 2
                    w(v, "T5", gidx + 1)
                    v.scalar_tensor_tensor(big[:, i % 2, u, :], ps_ra[sl][:, 0:TT],
                                           thrb[:, 1:2], partial[:, i % 2, u, :],
                                           OP.mult, OP.add).then_inc(sem["V9"], 1)
    return nc


def _cvt(vec):
    return np.ascontiguousarray(
        np.asarray(vec, np.float32).reshape(-1)[:C].reshape(CG, P).T)


def kernel(L0_attn_x, L0_attn_time_mix_k, L0_attn_time_mix_v, L0_attn_time_mix_r,
           L0_attn_time_decay, L0_attn_time_first, L0_attn_Wk, L0_attn_Wv,
           L0_attn_Wr, L0_attn_Wo, L0_attn_ln_g, L0_attn_ln_b, L0_ffn_weight,
           L0_ffn_ln_gamma, L0_ffn_ln_beta):
    x = np.asarray(L0_attn_x, np.float32)
    consts = np.ascontiguousarray(np.stack([
        _cvt(L0_attn_time_mix_k), _cvt(L0_attn_time_mix_v), _cvt(L0_attn_time_mix_r),
        _cvt(L0_attn_time_decay), _cvt(L0_attn_time_first),
        _cvt(L0_attn_ln_g), _cvt(L0_attn_ln_b),
        _cvt(L0_ffn_ln_gamma), _cvt(L0_ffn_ln_beta)]))
    wkt = np.ascontiguousarray(np.asarray(L0_attn_Wk, np.float32).T)
    wvt = np.ascontiguousarray(np.asarray(L0_attn_Wv, np.float32).T)
    wrt = np.ascontiguousarray(np.asarray(L0_attn_Wr, np.float32).T)
    wot = np.ascontiguousarray(np.asarray(L0_attn_Wo, np.float32).T)
    wffn = np.ascontiguousarray(np.asarray(L0_ffn_weight, np.float32))

    nc = build_nc()
    in_maps = []
    for ci in range(NCORES):
        xs = np.ascontiguousarray(x[NSEQ * ci:NSEQ * (ci + 1)].transpose(0, 2, 1))
        in_maps.append({"x": xs, "wkt": wkt, "wvt": wvt, "wrt": wrt, "wot": wot,
                        "wffn": wffn, "consts": consts})
    res = run_bass_kernel_spmd(nc, in_maps, core_ids=list(range(NCORES)))
    out = np.empty((NCORES * NSEQ, TSEQ, C), np.float32)
    for ci in range(NCORES):
        out[NSEQ * ci:NSEQ * (ci + 1)] = res.results[ci]["out"].transpose(0, 2, 1)
    return out


# version 19 (final)
# speedup vs baseline: 1.2952x; 1.0202x over previous
import sys
if "/opt/trn_rl_repo" not in sys.path:
    sys.path.insert(0, "/opt/trn_rl_repo")

from contextlib import ExitStack

import numpy as np

import concourse.bass as bass
import concourse.mybir as mybir
from concourse.bass_utils import run_bass_kernel_spmd

F32 = mybir.dt.float32
BF16 = mybir.dt.bfloat16
AF = mybir.ActivationFunctionType
OP = mybir.AluOpType
AX = mybir.AxisListType.X

P = 128
CG = 8
TT = 256
NSEQ = 2
TSEQ = 4096
NTPS = TSEQ // TT
NT = NSEQ * NTPS
C = 1024
NCORES = 8

INV_C = 1.0 / 1024.0
CTAU = (128.0 - 1e-5) / 128.0
I_TMK, I_TMV, I_TMR, I_TD, I_TF, I_LNG, I_LNB, I_FG, I_FB = range(9)


def _ap(handle, off, dims):
    a = handle.ap()
    return bass.AP(a.tensor, off, [list(d) for d in dims])


def _fl(ap3):
    return ap3.rearrange("p a b -> p (a b)")


def _bcast_t(col_ap, n):
    return bass.AP(col_ap.tensor, col_ap.offset, [list(col_ap.ap[0]), [0, n]])


def build_nc():
    nc = bass.Bass("TRN2", target_bir_lowering=False, detect_race_conditions=False)

    x_d = nc.dram_tensor("x", [NSEQ, C, TSEQ], F32, kind="ExternalInput")
    wkt_d = nc.dram_tensor("wkt", [C, C], F32, kind="ExternalInput")
    wvt_d = nc.dram_tensor("wvt", [C, C], F32, kind="ExternalInput")
    wrt_d = nc.dram_tensor("wrt", [C, C], F32, kind="ExternalInput")
    wot_d = nc.dram_tensor("wot", [C, C], F32, kind="ExternalInput")
    wffn_d = nc.dram_tensor("wffn", [C, C], F32, kind="ExternalInput")
    co_d = nc.dram_tensor("consts", [9, P, CG], F32, kind="ExternalInput")
    out_d = nc.dram_tensor("out", [NSEQ, C, TSEQ], F32, kind="ExternalOutput")

    sc_part = nc.dram_tensor("sc_part", [NT, P, CG * TT], F32)
    sc_xn2 = nc.dram_tensor("sc_xn2", [NT, P, CG * TT], BF16)
    colbuf = nc.dram_tensor("colbuf", [P], F32)
    cc_in = nc.dram_tensor("cc_in", [1], F32)
    cc_out = nc.dram_tensor("cc_out", [1], F32)

    with ExitStack() as es:
        e = es.enter_context
        wk_sb = e(nc.sbuf_tensor("wk_sb", [P, CG, C], BF16))
        wv_sb = e(nc.sbuf_tensor("wv_sb", [P, CG, C], BF16))
        wr_sb = e(nc.sbuf_tensor("wr_sb", [P, CG, C], BF16))
        wo_sb = e(nc.sbuf_tensor("wo_sb", [P, CG, C], BF16))
        wq_sb = e(nc.sbuf_tensor("wq_sb", [P, CG, C], BF16))
        stage = e(nc.sbuf_tensor("stage", [P, C], F32))
        xT = e(nc.sbuf_tensor("xT", [P, 2, CG, TT], F32))
        xs2 = e(nc.sbuf_tensor("xs2", [P, CG, 2, TT], BF16))
        xn2b = e(nc.sbuf_tensor("xn2b", [P, 2, CG, TT], BF16))
        partial = e(nc.sbuf_tensor("partial", [P, 2, CG, TT], F32))
        big = e(nc.sbuf_tensor("big", [P, 2, CG, TT], F32))
        xn = e(nc.sbuf_tensor("xn", [P, CG, TT], BF16))
        tmpb = e(nc.sbuf_tensor("tmpb", [P, CG, TT], BF16))
        xxs = e(nc.sbuf_tensor("xxs", [P, CG, TT + 2], BF16))
        xk = e(nc.sbuf_tensor("xk", [P, CG, TT], BF16))
        xv = e(nc.sbuf_tensor("xv", [P, CG, TT], BF16))
        xr = e(nc.sbuf_tensor("xr", [P, CG, TT], BF16))
        os2 = e(nc.sbuf_tensor("os2", [P, CG, 2, TT], BF16))
        vsb = e(nc.sbuf_tensor("vsb", [P, CG, TT], BF16))
        xln2 = e(nc.sbuf_tensor("xln2", [P, CG, TT], BF16))
        sums = e(nc.sbuf_tensor("sums", [P, 2, TT], F32))
        wsp = e(nc.sbuf_tensor("wsp", [P, 4, TT], F32))
        r13 = e(nc.sbuf_tensor("r13", [P, 2, TT], F32))
        inv13 = e(nc.sbuf_tensor("inv13", [P, 2, TT], F32))
        a1b = e(nc.sbuf_tensor("a1b", [P, TT], BF16))
        b1mb = e(nc.sbuf_tensor("b1mb", [P, TT], BF16))
        A3b = e(nc.sbuf_tensor("A3b", [P, TT], BF16))
        B3mb = e(nc.sbuf_tensor("B3mb", [P, TT], BF16))
        a2b = e(nc.sbuf_tensor("a2b", [P, TT], BF16))
        b2mb = e(nc.sbuf_tensor("b2mb", [P, TT], BF16))
        co = e(nc.sbuf_tensor("co", [P, 9, CG], F32))
        dec_sb = e(nc.sbuf_tensor("dec_sb", [P, CG], F32))
        ones128 = e(nc.sbuf_tensor("ones128", [P, P], BF16))
        onesrow = e(nc.sbuf_tensor("onesrow", [1, P], F32))
        carry = e(nc.sbuf_tensor("carry", [P, CG], F32))
        xnl2 = e(nc.sbuf_tensor("xnl2", [P, CG, 2], BF16))
        zb2 = e(nc.sbuf_tensor("zb2", [P, CG, 2], BF16))
        mxacc = e(nc.sbuf_tensor("mxacc", [P, 1], F32))
        mxall = e(nc.sbuf_tensor("mxall", [P, NT], F32))
        absrow = e(nc.sbuf_tensor("absrow", [P, 1], F32))
        absall = e(nc.sbuf_tensor("absall", [P, CG], F32))
        thrb = e(nc.sbuf_tensor("thrb", [P, 2], F32))
        thrneg = e(nc.sbuf_tensor("thrneg", [P, 1], F32))
        tauv = e(nc.sbuf_tensor("tauv", [P, 2], F32))
        mrow = e(nc.sbuf_tensor("mrow", [1, P], F32))
        tiny = e(nc.sbuf_tensor("tiny", [1, 8], F32))

        ps_stat = e(nc.psum_tensor("ps_stat", [P, 512], F32))
        ps_k0 = e(nc.psum_tensor("ps_k0", [P, 512], F32))
        ps_k1 = e(nc.psum_tensor("ps_k1", [P, 512], F32))
        ps_v0 = e(nc.psum_tensor("ps_v0", [P, 512], F32))
        ps_v1 = e(nc.psum_tensor("ps_v1", [P, 512], F32))
        ps_ra0 = e(nc.psum_tensor("ps_ra0", [P, 512], F32))
        ps_ra1 = e(nc.psum_tensor("ps_ra1", [P, 512], F32))
        ps_small = e(nc.psum_tensor("ps_small", [P, 512], F32))
        ps_k = [ps_k0, ps_k1]
        ps_v = [ps_v0, ps_v1]
        ps_ra = [ps_ra0, ps_ra1]

        names = ("X W WG B1 L2 SX SP SO G1 G2 CC BG "
                 "S1 S2 S3 S5 S6 S7 S8 SB SD T1 T1A T2 T2V T2K T3 T3A T4 T5 T6 TB VE1 VE2 "
                 "V0 V1 V2 V3 V4 V5 V6 V7 V8 V9 VO VBX VG1 VG2 VT B2 B4 B5 B6").split()
        sem = {n: e(nc.semaphore(n)) for n in names}
        block = e(nc.Block())

        def w(eng, name, val):
            if val > 0:
                eng.wait_ge(sem[name], val)

        def col(k, cg):
            return co[:, k, cg:cg + 1]

        def bc(tile):
            a = tile[:, :]
            return bass.AP(a.tensor, a.offset, [[a.ap[0][0], P], [0, CG], [1, TT]])

        def x_tile_ap(i):
            seq, ti = divmod(i, NTPS)
            off = seq * C * TSEQ + ti * TT
            return _ap(x_d, off, [[TSEQ, P], [P * TSEQ, CG], [1, TT]])

        def out_tile_ap(i):
            seq, ti = divmod(i, NTPS)
            off = seq * C * TSEQ + ti * TT
            return _ap(out_d, off, [[TSEQ, P], [P * TSEQ, CG], [1, TT]])

        def scr_ap(handle, i):
            return _ap(handle, i * P * CG * TT, [[CG * TT, P], [1, CG * TT]])

        def wchunk_ap(handle, c_):
            return _ap(handle, c_ * P * C, [[C, P], [1, C]])

        @block.sync
        def _(sync):
            sync.dma_start(out=co[:, :, :],
                           in_=_ap(co_d, 0, [[CG, P], [P * CG, 9], [1, CG]])
                           ).then_inc(sem["W"], 16)
            for i in range(NT):
                w(sync, "V6", 8 * (i - 1))
                sync.dma_start(out=xT[:, i % 2, :, :], in_=x_tile_ap(i)
                               ).then_inc(sem["X"], 16)
            sync.wait_ge(sem["SX"], 16 * NT)
            sync.wait_ge(sem["SP"], 16 * NT)
            for i in range(NT):
                w(sync, "V8", i - 1)
                sync.dma_start(out=_fl(xn2b[:, i % 2, :, :]), in_=scr_ap(sc_xn2, i)
                               ).then_inc(sem["L2"], 16)
                w(sync, "V9", 8 * (i - 1))
                sync.dma_start(out=_fl(partial[:, i % 2, :, :]), in_=scr_ap(sc_part, i)
                               ).then_inc(sem["L2"], 16)

        @block.gpsimd
        def _(g):
            for wsb, wd in ((wk_sb, wkt_d), (wv_sb, wvt_d), (wr_sb, wrt_d),
                            (wo_sb, wot_d)):
                g.dma_start(out=wsb[:, :, :],
                            in_=_ap(wd, 0, [[C, P], [P * C, CG], [1, C]])
                            ).then_inc(sem["WG"], 16)
            for c_ in range(CG):
                w(g, "B2", c_)
                g.dma_start(out=stage[:, :], in_=wchunk_ap(wffn_d, c_)
                            ).then_inc(sem["B1"], 16)
            g.wait_ge(sem["B2"], CG + 1)
            g.dma_start(out=colbuf.ap(), in_=absrow[:, 0:1]).then_inc(sem["BG"], 16)
            g.wait_ge(sem["BG"], 16)
            g.dma_start(out=mrow[0:1, :],
                        in_=_ap(colbuf, 0, [[P, 1], [1, P]])).then_inc(sem["BG"], 16)
            for c_ in range(CG):
                w(g, "B6", c_)
                g.dma_start(out=stage[:, :], in_=wchunk_ap(wffn_d, c_)
                            ).then_inc(sem["B1"], 16)
            def g_xsq(j):
                w(g, "S1", j + 1)
                w(g, "V2", j)
                w(g, "T1", j)
                g.tensor_mul(xs2[:, :, 1, :], xs2[:, :, 0, :], xs2[:, :, 0, :]
                             ).then_inc(sem["V0"], 1)

            g_xsq(0)
            for i in range(NT):
                if i + 1 < NT:
                    g_xsq(i + 1)
                w(g, "V7", i + 1)
                g.dma_start(out=scr_ap(sc_xn2, i), in_=_fl(xn2b[:, i % 2, :, :])
                            ).then_inc(sem["SX"], 16)
                w(g, "V3", 8 * (i + 1))
                w(g, "T3", i)
                g.tensor_mul(os2[:, :, 1, :], os2[:, :, 0, :], os2[:, :, 0, :]
                             ).then_inc(sem["VO"], 1)
                w(g, "V6", 8 * (i + 1))
                g.dma_start(out=scr_ap(sc_part, i), in_=_fl(partial[:, i % 2, :, :])
                            ).then_inc(sem["SP"], 16)
            g.wait_ge(sem["VG1"], 1)
            g.dma_start(out=colbuf.ap(), in_=mxacc[:, 0:1]).then_inc(sem["G1"], 16)
            g.wait_ge(sem["G1"], 16)
            g.dma_start(out=mrow[0:1, :],
                        in_=_ap(colbuf, 0, [[P, 1], [1, P]])).then_inc(sem["G1"], 16)
            g.wait_ge(sem["VG2"], 1)
            g.dma_start(out=cc_in.ap(), in_=tiny[0:1, 3:4]).then_inc(sem["G1"], 16)
            g.wait_ge(sem["G1"], 48)
            g.collective_compute(
                "AllReduce", OP.max,
                replica_groups=[list(range(NCORES))],
                ins=[cc_in.ap()],
                outs=[cc_out.ap()],
            ).then_inc(sem["CC"], 1)
            g.wait_ge(sem["CC"], 1)
            g.dma_start(out=tiny[0:1, 4:5], in_=cc_out.ap()).then_inc(sem["G2"], 16)
            for i in range(NT):
                w(g, "V9", 8 * (i + 1))
                g.dma_start(out=out_tile_ap(i), in_=big[:, i % 2, :, :]
                            ).then_inc(sem["SO"], 16)
            g.wait_ge(sem["SO"], 16 * NT)

        @block.scalar
        def _(s):
            s.preload_act_table(AF.Exp)
            s.wait_ge(sem["W"], 16)
            s.activation(dec_sb[:, :], co[:, I_TD, :], AF.Exp
                         ).then_inc(sem["SD"], 1)
            s.preload_act_table(AF.Sqrt)
            s.wait_ge(sem["TB"], 1)
            s.copy(thrb[:, :], ps_small[:, 2:4]).then_inc(sem["SB"], 1)

            def s_cast(i):
                w(s, "X", 16 * (i + 1))
                w(s, "T1", i)
                w(s, "V7", i)
                s.copy(xs2[:, :, 0, :], xT[:, i % 2, :, :]
                       ).then_inc(sem["S1"], 1)

            def s_sqrt13(i):
                w(s, "V1", i + 1)
                s.activation(_fl(inv13[:, :, :]), _fl(r13[:, :, :]), AF.Sqrt
                             ).then_inc(sem["S3"], 1)

            def s_sqrt2(i):
                w(s, "V4", i + 1)
                s.activation(inv13[:, 0, :], r13[:, 0, :], AF.Sqrt
                             ).then_inc(sem["S6"], 1)

            s_cast(0)
            s_sqrt13(0)
            for i in range(NT - 1):
                s_cast(i + 1)
                s_sqrt13(i + 1)
                s_sqrt2(i)
            s_sqrt2(NT - 1)
            s.wait_ge(sem["T6"], 1)
            s.copy(tauv[:, :], ps_small[:, 0:2]).then_inc(sem["S7"], 1)

        @block.tensor
        def _(t):
            t.wait_ge(sem["B4"], 1)
            t.matmul(ps_small[:, 2:4], onesrow[0:1, :], tiny[0:1, 1:3],
                     start=True, stop=True).then_inc(sem["TB"], 1)
            t.wait_ge(sem["WG"], 64)

            def t_stats1(j):
                w(t, "S1", j + 1)
                w(t, "V0", j + 1)
                w(t, "S5", j - 1)
                for cg in range(CG):
                    mm = t.matmul(ps_stat[:, 0:512], ones128[:, :],
                                  xs2[:, cg, :, :].rearrange("p a b -> p (a b)"),
                                  start=(cg == 0), stop=(cg == CG - 1))
                mm.then_inc(sem["T1"], 1)

            def t_proj(j):
                w(t, "V2", j + 1)
                for d in range(CG):
                    gidx = CG * j + d
                    sl = gidx % 2
                    w(t, "V3", gidx - 1)
                    if j >= 1:
                        w(t, "V6", 8 * j - 1 + sl)
                    for cg in range(CG):
                        mm = t.matmul(ps_v[sl][:, 0:TT], wv_sb[:, cg, d * P:(d + 1) * P],
                                      xv[:, cg, :], start=(cg == 0), stop=(cg == CG - 1))
                    mm.then_inc(sem["T2V"], 1)
                    for cg in range(CG):
                        mm = t.matmul(ps_k[sl][:, 0:TT], wk_sb[:, cg, d * P:(d + 1) * P],
                                 xk[:, cg, :], start=(cg == 0), stop=(cg == CG - 1))
                    mm.then_inc(sem["T2K"], 1)
                    for cg in range(CG):
                        mm = t.matmul(ps_ra[sl][:, 0:TT], wr_sb[:, cg, d * P:(d + 1) * P],
                                      xr[:, cg, :], start=(cg == 0), stop=(cg == CG - 1))
                    mm.then_inc(sem["T2"], 1)

            def t_stats2(i):
                w(t, "VO", i + 1)
                w(t, "S2", min(i + 2, NT))
                for cg in range(CG):
                    mm = t.matmul(ps_stat[:, 0:512], ones128[:, :],
                                  os2[:, cg, :, :].rearrange("p a b -> p (a b)"),
                                  start=(cg == 0), stop=(cg == CG - 1))
                mm.then_inc(sem["T3"], 1)

            def t_wo(i):
                w(t, "V5", i + 1)
                for d in range(CG):
                    gidx = CG * i + d
                    sl = gidx % 2
                    w(t, "V6", gidx - 1)
                    for cg in range(CG):
                        mm = t.matmul(ps_ra[sl][:, 0:TT], wo_sb[:, cg, d * P:(d + 1) * P],
                                      xln2[:, cg, :], start=(cg == 0), stop=(cg == CG - 1))
                    mm.then_inc(sem["T4"], 1)

            t_stats1(0)
            t_proj(0)
            for i in range(NT - 1):
                t_stats1(i + 1)
                t_stats2(i)
                t_wo(i)
                t_proj(i + 1)
            t_stats2(NT - 1)
            t_wo(NT - 1)
            t.wait_ge(sem["VT"], 1)
            t.matmul(ps_small[:, 0:2], onesrow[0:1, :], tiny[0:1, 5:7],
                     start=True, stop=True).then_inc(sem["T6"], 1)
            t.wait_ge(sem["V6"], 8 * NT)
            for i in range(NT):
                w(t, "V8", i + 1)
                for u in range(CG):
                    gidx = CG * i + u
                    sl = gidx % 2
                    w(t, "V9", gidx - 1)
                    for dcg in range(CG):
                        mm = t.matmul(ps_ra[sl][:, 0:TT], wq_sb[:, dcg, u * P:(u + 1) * P],
                                      xs2[:, dcg, 0, :], start=(dcg == 0), stop=(dcg == CG - 1))
                    mm.then_inc(sem["T5"], 1)

        @block.vector
        def _(v):
            v.memset(ones128[:, :], 1.0)
            v.memset(onesrow[0:1, :], 1.0)
            v.memset(zb2[:, :, :], 0.0)
            v.memset(mxall[:, :], 0.0)
            for c_ in range(CG):
                w(v, "B1", 16 * (c_ + 1))
                v.tensor_reduce(absall[:, c_:c_ + 1], stage[:, :], AX, OP.add,
                                apply_absolute_value=True).then_inc(sem["B2"], 1)
            v.drain()
            v.tensor_reduce(absrow[:, :], absall[:, :], AX, OP.add)
            v.drain()
            v.sem_inc(sem["B2"], 1)
            v.wait_ge(sem["BG"], 32)
            v.tensor_reduce(tiny[0:1, 0:1], mrow[0:1, :], AX, OP.add)
            v.drain()
            v.tensor_scalar_mul(tiny[0:1, 7:8], tiny[0:1, 0:1], 0.5 / 1048576.0)
            v.tensor_scalar_mul(tiny[0:1, 2:3], tiny[0:1, 0:1], 1.0 / 1048576.0)
            v.drain()
            v.tensor_scalar_add(tiny[0:1, 1:2], tiny[0:1, 7:8], 5e-6)
            v.drain()
            v.sem_inc(sem["B4"], 1)
            v.wait_ge(sem["SB"], 1)
            v.tensor_scalar_mul(thrneg[:, :], thrb[:, 0:1], -1.0)
            v.drain()
            for c_ in range(CG):
                w(v, "B1", 16 * (CG + c_ + 1))
                gv = xk[:, 0:4, :].rearrange("p a b -> p (a b)")
                lv = xk[:, 4:8, :].rearrange("p a b -> p (a b)")
                v.tensor_single_scalar(gv, stage[:, :], thrb[:, 0:1], OP.is_gt)
                v.tensor_single_scalar(lv, stage[:, :], thrneg[:, 0:1], OP.is_lt)
                v.tensor_sub(wq_sb[:, c_, :], gv, lv).then_inc(sem["B6"], 1)
            v.wait_ge(sem["W"], 16)
            v.wait_ge(sem["SD"], 1)

            def v_A(i):
                w(v, "T1", i + 1)
                v.tensor_copy(_fl(sums[:, :, :]), ps_stat[:, 0:512]
                              ).then_inc(sem["S2"], 1)
                v.tensor_scalar_mul(wsp[:, 0, :], sums[:, 0, :], INV_C)
                v.tensor_scalar_mul(wsp[:, 1, :], sums[:, 1, :], INV_C)
                v.tensor_mul(wsp[:, 2, :], wsp[:, 0, :], wsp[:, 0, :])
                v.tensor_sub(wsp[:, 1, :], wsp[:, 1, :], wsp[:, 2, :])
                v.tensor_scalar_add(wsp[:, 2, :], wsp[:, 1, :], 1e-5)
                v.reciprocal(r13[:, 0, :], wsp[:, 2, :])
                v.tensor_mul(wsp[:, 3, :], wsp[:, 1, :], r13[:, 0, :])
                v.tensor_scalar_add(wsp[:, 3, :], wsp[:, 3, :], 1e-3)
                v.reciprocal(r13[:, 1, :], wsp[:, 3, :]).then_inc(sem["V1"], 1)
                w(v, "S3", i + 1)
                v.tensor_copy(a1b[:, :], inv13[:, 0, :])
                v.tensor_mul(wsp[:, 2, :], wsp[:, 0, :], inv13[:, 0, :])
                v.tensor_copy(b1mb[:, :], wsp[:, 2, :])
                v.tensor_mul(A3b[:, :], inv13[:, 0, :], inv13[:, 1, :])
                v.tensor_mul(B3mb[:, :], wsp[:, 2, :], inv13[:, 1, :])

            def v_B(i):
                ti = i % NTPS
                v.tensor_mul(_fl(tmpb[:, :, :]), xs2[:, :, 0, :], bc(a1b))
                v.tensor_sub(_fl(xn[:, :, :]), _fl(tmpb[:, :, :]), bc(b1mb))
                w(v, "SX", 16 * (i - 1))
                v.tensor_mul(_fl(tmpb[:, :, :]), xs2[:, :, 0, :], bc(A3b))
                v.tensor_sub(_fl(tmpb[:, :, :]), _fl(tmpb[:, :, :]), bc(B3mb))
                for cg in range(CG):
                    mi = v.tensor_scalar(xn2b[:, i % 2, cg, :], tmpb[:, cg, :],
                                         col(I_FG, cg), col(I_FB, cg), OP.mult, OP.add)
                mi.then_inc(sem["V7"], 1)
                v.tensor_reduce(mxall[:, i:i + 1], _fl(xn2b[:, i % 2, :, :]), AX,
                                OP.max, apply_absolute_value=True)
                bsrc = zb2 if ti == 0 else xnl2
                for cg in range(CG):
                    v.tensor_copy(xxs[:, cg, 0:2], bsrc[:, cg, :])
                v.drain()
                for cg in range(CG):
                    v.tensor_copy(xxs[:, cg, 2:TT + 2], xn[:, cg, 0:TT])
                ln = xn[:, :, TT - 1:TT]
                v.tensor_copy(xnl2[:, :, :],
                              bass.AP(ln.tensor, ln.offset,
                                      [list(ln.ap[0]), list(ln.ap[1]), [0, 2]]))
                w(v, "T1", i + 1)
                for cg in range(CG):
                    v.tensor_sub(xs2[:, cg, 1, :], xn[:, cg, :], xxs[:, cg, 1:TT + 1])
                for kcol, buf in ((I_TMK, xk), (I_TMV, xv), (I_TMR, xr)):
                    for cg in range(CG):
                        mi = v.scalar_tensor_tensor(buf[:, cg, :], xs2[:, cg, 1, :],
                                                    col(kcol, cg), xxs[:, cg, 1:TT + 1],
                                                    OP.mult, OP.add)
                mi.then_inc(sem["V2"], 1)

            def v_C(i):
                ti = i % NTPS
                w(v, "T3", i)
                for d in range(CG):
                    gidx = CG * i + d
                    sl = gidx % 2
                    w(v, "T2V", gidx + 1)
                    kv_d = big[:, 0, d, :]
                    st_d = big[:, 1, d, :]
                    v.tensor_copy(vsb[:, d, :], ps_v[sl][:, 0:TT])
                    w(v, "T2K", gidx + 1)
                    v.tensor_mul(kv_d, ps_k[sl][:, 0:TT], vsb[:, d, :])
                    init = 0.0 if ti == 0 else carry[:, d:d + 1]
                    v.tensor_tensor_scan(st_d, _bcast_t(dec_sb[:, d:d + 1], TT),
                                         kv_d, init, OP.mult, OP.add)
                    v.scalar_tensor_tensor(kv_d, kv_d, col(I_TF, d),
                                           st_d, OP.mult, OP.add)
                    w(v, "T2", gidx + 1)
                    v.tensor_mul(os2[:, d, 0, :], ps_ra[sl][:, 0:TT], kv_d
                                 ).then_inc(sem["V3"], 1)
                v.tensor_copy(carry[:, :], _fl(big[:, 1, :, TT - 1:TT]))

            def v_D(i):
                w(v, "T3", i + 1)
                v.tensor_copy(_fl(sums[:, :, :]), ps_stat[:, 0:512]
                              ).then_inc(sem["S5"], 1)
                v.tensor_scalar_mul(wsp[:, 0, :], sums[:, 0, :], INV_C)
                v.tensor_scalar_mul(wsp[:, 1, :], sums[:, 1, :], INV_C)
                v.tensor_mul(wsp[:, 2, :], wsp[:, 0, :], wsp[:, 0, :])
                v.tensor_sub(wsp[:, 1, :], wsp[:, 1, :], wsp[:, 2, :])
                v.tensor_scalar_add(wsp[:, 2, :], wsp[:, 1, :], 1e-5)
                v.reciprocal(r13[:, 0, :], wsp[:, 2, :]).then_inc(sem["V4"], 1)
                w(v, "S6", i + 1)
                v.tensor_copy(a2b[:, :], inv13[:, 0, :])
                v.tensor_mul(wsp[:, 2, :], wsp[:, 0, :], inv13[:, 0, :])
                v.tensor_copy(b2mb[:, :], wsp[:, 2, :])
                v.tensor_mul(_fl(tmpb[:, :, :]), os2[:, :, 0, :], bc(a2b))
                v.tensor_sub(_fl(tmpb[:, :, :]), _fl(tmpb[:, :, :]), bc(b2mb))
                for cg in range(CG):
                    mi = v.tensor_scalar(xln2[:, cg, :], tmpb[:, cg, :],
                                         col(I_LNG, cg), col(I_LNB, cg), OP.mult, OP.add)
                mi.then_inc(sem["V5"], 1)

            def v_E(i):
                w(v, "SP", 16 * (i - 1))
                for d in range(CG):
                    gidx = CG * i + d
                    sl = gidx % 2
                    w(v, "T4", gidx + 1)
                    v.tensor_add(partial[:, i % 2, d, :], ps_ra[sl][:, 0:TT],
                                 xT[:, i % 2, d, :]).then_inc(sem["V6"], 1)

            def v_gamma_start():
                v.drain()
                v.tensor_reduce(mxacc[:, :], mxall[:, 0:NT], AX, OP.max)
                v.drain()
                v.sem_inc(sem["VG1"], 1)

            v_A(0)
            v_B(0)
            v_C(0)
            for i in range(NT - 1):
                v_A(i + 1)
                v_D(i)
                v_B(i + 1)
                if i == NT - 2:
                    v_gamma_start()
                v_E(i)
                v_C(i + 1)
            v_D(NT - 1)
            v_E(NT - 1)

            v.wait_ge(sem["G1"], 32)
            v.tensor_reduce(tiny[0:1, 3:4], mrow[0:1, :], AX, OP.max)
            v.drain()
            v.sem_inc(sem["VG2"], 1)
            v.wait_ge(sem["G2"], 16)
            v.tensor_scalar_mul(tiny[0:1, 5:6], tiny[0:1, 4:5], CTAU)
            v.tensor_scalar_mul(tiny[0:1, 6:7], tiny[0:1, 4:5], -CTAU)
            v.drain()
            v.sem_inc(sem["VT"], 1)
            v.wait_ge(sem["S7"], 1)
            w(v, "T1", NT)

            for i in range(NT):
                w(v, "L2", 32 * i + 16)
                v.tensor_scalar(xs2[:, :, 0, :], _fl(xn2b[:, i % 2, :, :]),
                                tauv[:, 1:2], tauv[:, 0:1], OP.max, OP.min
                                ).then_inc(sem["V8"], 1)
                w(v, "L2", 32 * (i + 1))
                w(v, "SO", 16 * (i - 1))
                for u in range(CG):
                    gidx = CG * i + u
                    sl = gidx % 2
                    w(v, "T5", gidx + 1)
                    v.scalar_tensor_tensor(big[:, i % 2, u, :], ps_ra[sl][:, 0:TT],
                                           thrb[:, 1:2], partial[:, i % 2, u, :],
                                           OP.mult, OP.add).then_inc(sem["V9"], 1)
    return nc


def _cvt(vec):
    return np.ascontiguousarray(
        np.asarray(vec, np.float32).reshape(-1)[:C].reshape(CG, P).T)


def kernel(L0_attn_x, L0_attn_time_mix_k, L0_attn_time_mix_v, L0_attn_time_mix_r,
           L0_attn_time_decay, L0_attn_time_first, L0_attn_Wk, L0_attn_Wv,
           L0_attn_Wr, L0_attn_Wo, L0_attn_ln_g, L0_attn_ln_b, L0_ffn_weight,
           L0_ffn_ln_gamma, L0_ffn_ln_beta):
    x = np.asarray(L0_attn_x, np.float32)
    consts = np.ascontiguousarray(np.stack([
        _cvt(L0_attn_time_mix_k), _cvt(L0_attn_time_mix_v), _cvt(L0_attn_time_mix_r),
        _cvt(L0_attn_time_decay), _cvt(L0_attn_time_first),
        _cvt(L0_attn_ln_g), _cvt(L0_attn_ln_b),
        _cvt(L0_ffn_ln_gamma), _cvt(L0_ffn_ln_beta)]))
    wkt = np.ascontiguousarray(np.asarray(L0_attn_Wk, np.float32).T)
    wvt = np.ascontiguousarray(np.asarray(L0_attn_Wv, np.float32).T)
    wrt = np.ascontiguousarray(np.asarray(L0_attn_Wr, np.float32).T)
    wot = np.ascontiguousarray(np.asarray(L0_attn_Wo, np.float32).T)
    wffn = np.ascontiguousarray(np.asarray(L0_ffn_weight, np.float32))

    nc = build_nc()
    in_maps = []
    for ci in range(NCORES):
        xs = np.ascontiguousarray(x[NSEQ * ci:NSEQ * (ci + 1)].transpose(0, 2, 1))
        in_maps.append({"x": xs, "wkt": wkt, "wvt": wvt, "wrt": wrt, "wot": wot,
                        "wffn": wffn, "consts": consts})
    res = run_bass_kernel_spmd(nc, in_maps, core_ids=list(range(NCORES)))
    out = np.empty((NCORES * NSEQ, TSEQ, C), np.float32)
    for ci in range(NCORES):
        out[NSEQ * ci:NSEQ * (ci + 1)] = res.results[ci]["out"].transpose(0, 2, 1)
    return out
